# version 26
# baseline (speedup 1.0000x reference)
import numpy as np
import concourse.bass as bass
import concourse.mybir as mybir
from concourse.bass_utils import run_bass_kernel_spmd

F32 = mybir.dt.float32
BF16 = mybir.dt.bfloat16
I32 = mybir.dt.int32
AF = mybir.ActivationFunctionType
ALU = mybir.AluOpType
AX = mybir.AxisListType

NC = 8
D = 1024
DC = 8
II = 2816
IC = 22
TB = 1024
NB = 2
MAGIC_K = float(1.5 * (2.0**82 - 2.0**53))
MAGIC_LO = float(1.5 * 2.0**23)

ENGINES = ("sync", "vector", "scalar", "tensor", "gpsimd")
V, S, T, G, Y = "vector", "scalar", "tensor", "gpsimd", "sync"


class Buf:
    __slots__ = ("wr", "rd")

    def __init__(self):
        self.wr = None
        self.rd = []

    def rwaits(self):
        return [self.wr] if self.wr else []

    def wwaits(self):
        return ([self.wr] if self.wr else []) + self.rd

    def did_r(self, tok):
        self.rd.append(tok)

    def did_w(self, tok):
        self.wr = tok
        self.rd = []


DMA_ROT = {"sync": 8, "gpsimd": 6}


class Prog:
    def __init__(self, nc):
        self.nc = nc
        self.qs = {e: [] for e in ENGINES}
        self.cnt = {}
        self.handles = {}
        self.watermark = {e: {} for e in ENGINES}
        self.sem_names = []
        self.dmarr = {e: 0 for e in ENGINES}

    def sem(self, name):
        self.cnt[name] = 0
        self.sem_names.append(name)

    def _waits(self, eng, waits):
        merged = {}
        for w in waits:
            if w is None:
                continue
            s, v = w
            if v <= 0:
                continue
            if merged.get(s, -1) < v:
                merged[s] = v
        for s, v in sorted(merged.items()):
            if self.watermark[eng].get(s, -1) >= v:
                continue
            self.watermark[eng][s] = v
            self.qs[eng].append(lambda e, s=s, v=v: e.wait_ge(self.handles[s], v))

    def op(self, eng, build, reads=(), writes=(), extra=(), sem=None, n=1):
        waits = list(extra)
        for b in reads:
            waits += b.rwaits()
        for b in writes:
            waits += b.wwaits()
        self._waits(eng, waits)
        s = sem if sem is not None else eng
        self.cnt[s] += n
        tok = (s, self.cnt[s])
        self.qs[eng].append(lambda e, b=build, s=s, n=n: b(e).then_inc(self.handles[s], n))
        for b in reads:
            b.did_r(tok)
        for b in writes:
            b.did_w(tok)
        return tok

    def dma(self, eng, out, in_, reads=(), writes=(), extra=(), sem=None, slow=False):
        if slow:
            bld = lambda e, o=out, i=in_: e.dma_start(out=o, in_=i,
                                                      allow_slow_non_contiguous=True)
        else:
            bld = lambda e, o=out, i=in_: e.dma_start(out=o, in_=i)
        if sem is None:
            r = DMA_ROT[eng]
            s = "%sq%d" % (eng[:2], self.dmarr[eng] % r)
            self.dmarr[eng] += 1
            self._waits(eng, [(s, self.cnt[s])])
            sem = s
        return self.op(eng, bld, reads=reads, writes=writes, extra=extra, sem=sem, n=16)

    def mark(self, eng):
        return (eng, self.cnt[eng])

    def barrier(self):
        toks = [(s, self.cnt[s]) for s in self.sem_names]
        for e in ENGINES:
            self._waits(e, toks)


def build_program():
    nc = bass.Bass("TRN2", target_bir_lowering=False, debug=False)
    ap = lambda t: t.ap()
    x_d = ap(nc.dram_tensor("x", [NB * D, TB], F32, kind="ExternalInput"))
    wqT_d = ap(nc.dram_tensor("wqT", [D, D], F32, kind="ExternalInput"))
    wkT_d = ap(nc.dram_tensor("wkT", [D, D], F32, kind="ExternalInput"))
    wvT_d = ap(nc.dram_tensor("wvT", [D, D], F32, kind="ExternalInput"))
    woT_d = ap(nc.dram_tensor("woT", [D, D], F32, kind="ExternalInput"))
    bq_d = ap(nc.dram_tensor("bq", [1, D], F32, kind="ExternalInput"))
    bk_d = ap(nc.dram_tensor("bk", [1, D], F32, kind="ExternalInput"))
    bv_d = ap(nc.dram_tensor("bv", [1, D], F32, kind="ExternalInput"))
    bo_d = ap(nc.dram_tensor("bo", [1, D], F32, kind="ExternalInput"))
    wg_d = ap(nc.dram_tensor("wg", [II, D], F32, kind="ExternalInput"))
    wu_d = ap(nc.dram_tensor("wu", [II, D], F32, kind="ExternalInput"))
    wdn_d = ap(nc.dram_tensor("wdn", [D, II], F32, kind="ExternalInput"))
    cT_d = ap(nc.dram_tensor("cT", [128, NB], F32, kind="ExternalInput"))
    wmods_d = ap(nc.dram_tensor("wmods", [128, 6 * D], F32, kind="ExternalInput"))
    clipx_d = ap(nc.dram_tensor("clipx", [1, 1], F32, kind="ExternalInput"))
    clipd_d = ap(nc.dram_tensor("clipd", [1, 1], F32, kind="ExternalInput"))
    out_d = ap(nc.dram_tensor("out", [NB * D, TB], F32, kind="ExternalOutput"))

    ar0i = ap(nc.dram_tensor("ar0i", [128, 96], F32))
    ar0o = ap(nc.dram_tensor("ar0o", [128, 96], F32))
    kvari = ap(nc.dram_tensor("kvari", [NB * DC * 128, 1025], BF16))
    kvaro = ap(nc.dram_tensor("kvaro", [NB * DC * 128, 1025], BF16))
    ar2i = ap(nc.dram_tensor("ar2i", [1, 1], F32))
    ar2o = ap(nc.dram_tensor("ar2o", [1, 1], F32))
    ar3i = ap(nc.dram_tensor("ar3i", [1, 1], F32))
    ar3o = ap(nc.dram_tensor("ar3o", [1, 1], F32))
    rows_d = ap(nc.dram_tensor("rows_d", [16, TB], F32))
    scal_d = ap(nc.dram_tensor("scal_d", [8, 1], F32))
    x2sp = ap(nc.dram_tensor("x2sp", [NB * D, TB], F32))
    sisp = ap(nc.dram_tensor("sisp", [NB * IC * 128, TB], BF16))

    P = Prog(nc)
    for s in ("sync", "vector", "scalar", "tensor", "gpsimd", "coll"):
        P.sem(s)
    for e, r in DMA_ROT.items():
        for i in range(r):
            P.sem("%sq%d" % (e[:2], i))

    from contextlib import ExitStack
    with ExitStack() as stk:
        sbt = lambda name, shape, dt: stk.enter_context(nc.sbuf_tensor(name, shape, dt))
        xr_t = [sbt(f"xr{i}", [128, TB], F32) for i in range(3)]
        abA_t = sbt("abA", [128, TB], F32)
        abB_t = sbt("abB", [128, TB], F32)
        rz_t = sbt("rz", [128, TB], F32)
        big_t = sbt("big", [128, 5632], F32)
        t512_t = [sbt(f"t512_{i}", [128, 512], F32) for i in range(6)]
        t1k_t = [sbt(f"t1k_{i}", [128, TB], F32) for i in range(3)]
        kvt_t = [sbt(f"kvt{i}", [128, 1025], BF16) for i in range(2)]
        rows_t = sbt("rows", [1, 4 * TB + 2], F32)
        smallc_t = sbt("smallc", [128, 64], F32)
        ident_t = sbt("ident", [128, 128], F32)
        io1_t = sbt("io1", [128, 128], I32)
        io2_t = sbt("io2", [128, 128], I32)
        modp_t = sbt("modp", [128, 96], F32)
        modc_t = sbt("modc", [128, 96], F32)
        siluc_t = sbt("siluc", [128, 4], F32)
        mcol_t = sbt("mcol", [128, 128], F32)
        mxb_t = sbt("mxb", [128, 128], F32)
        rbh_t = [sbt(f"rbh{i}", [128, TB], BF16) for i in range(16)]
        rbk_t = [sbt(f"rbk{i}", [128, TB], BF16) for i in range(16)]
        rbw_t = [sbt(f"rbw{i}", [128, TB], BF16) for i in range(16)]
        rbx_t = sbt("rbx", [128, 4800], BF16)
        bsc_t = [sbt(f"bsc{i}", [128, TB], BF16) for i in range(2)]
        psA = [stk.enter_context(nc.psum_tensor(f"pA{k}", [128, 1024], F32))
               for k in range(3)]
        psT = [stk.enter_context(nc.psum_tensor(f"pT{k}", [128, 512], F32))
               for k in range(2)]
        psb = [psA[0][:, 0:512], psA[0][:, 512:1024],
               psA[1][:, 0:512], psA[1][:, 512:1024],
               psA[2][:, 0:512], psA[2][:, 512:1024],
               psT[0][:, 0:512], psT[1][:, 0:512]]
        for s in P.sem_names:
            P.handles[s] = stk.enter_context(nc.semaphore(f"sem_{s}"))
        block = stk.enter_context(nc.Block())

        B_xr = [Buf() for _ in range(3)]
        B_abA, B_abB, B_rz = Buf(), Buf(), Buf()
        big_guard = [[]]

        def big_bufs(k):
            toks = list(big_guard[0])
            bufs = [Buf() for _ in range(k)]
            return bufs, toks

        def big_close(bufs):
            toks = []
            for b in bufs:
                toks += b.wwaits()
            big_guard[0] = toks

        B_t512 = [Buf() for _ in range(6)]
        B_t1k = [Buf() for _ in range(3)]
        B_kvt = [Buf() for _ in range(2)]
        B_rows = {}

        def rowbuf(i):
            if i not in B_rows:
                B_rows[i] = Buf()
            return B_rows[i]

        B_col = {}
        COLIDX = {}

        def col(name):
            if name not in COLIDX:
                COLIDX[name] = len(COLIDX)
                B_col[name] = Buf()
            i = COLIDX[name]
            return smallc_t[:, i:i + 1], B_col[name]

        B_ident, B_io1, B_io2 = Buf(), Buf(), Buf()
        B_modp, B_modc, B_siluc, B_mcol, B_mxb = Buf(), Buf(), Buf(), Buf(), Buf()
        B_rbh = [Buf() for _ in range(16)]
        B_rbk = [Buf() for _ in range(16)]
        B_rbw = [Buf() for _ in range(16)]
        B_bq, B_bk, B_bv, B_bo = Buf(), Buf(), Buf(), Buf()
        B_o512, B_o128, B_oc = Buf(), Buf(), Buf()
        B_ks = [Buf() for _ in range(DC)]
        B_bsc = [Buf() for _ in range(2)]
        B_ps = [Buf() for _ in range(8)]
        B_x2d = [Buf() for _ in range(NB * DC)]
        B_sid = [Buf() for _ in range(NB * IC)]
        B_rowd = [Buf() for _ in range(16)]
        B_scald = [Buf() for _ in range(8)]
        B_ar = {k: Buf() for k in ("0i", "0o", "1i", "1o", "2i", "2o", "3i", "3o")}

        bq_r = rbx_t[0:1, 0:1024]
        bk_r = rbx_t[0:1, 1024:2048]
        bv_r = rbx_t[0:1, 2048:3072]
        bo_r = rbx_t[0:1, 3072:4096]
        ones_r512 = rbx_t[0:1, 4096:4608]
        ones_r128 = rbx_t[0:1, 4608:4736]
        ones_c = rbx_t[:, 4740:4741]
        ks_cs = [rbx_t[:, 4750 + i:4751 + i] for i in range(DC)]

        magic, B_magic = col("magic")
        eps_c, B_eps = col("eps")
        sigx_c, B_sigx = col("sigx")
        sigd_c, B_sigd = col("sigd")
        rs2_c, B_rs2 = col("rs2")
        s2_c, B_s2 = col("s2")
        rs3_c, B_rs3 = col("rs3")
        s3_c, B_s3 = col("s3")
        kst_c, B_kst = col("kst")
        tsc_c, B_tsc = col("tsc")

        def row(i, n=TB):
            return rows_t[0:1, i * TB:i * TB + n]

        def mcolv(g, b, c):
            j = (g * 8 + c) * 2 + b
            return modc_t[:, j:j + 1]

        P.op(G, lambda e: e.iota(out=io1_t[:], pattern=[[0, 128]], channel_multiplier=1),
             writes=[B_io1])
        P.op(G, lambda e: e.iota(out=io2_t[:], pattern=[[1, 128]], channel_multiplier=0),
             writes=[B_io2])
        P.op(V, lambda e: e.tensor_tensor(out=ident_t[:], in0=io1_t[:], in1=io2_t[:],
                                          op=ALU.is_equal),
             reads=[B_io1, B_io2], writes=[B_ident])
        P.op(V, lambda e: e.memset(mxb_t[:], 0.0), writes=[B_mxb])
        P.op(V, lambda e: e.memset(mcol_t[:], 0.0), writes=[B_mcol])
        P.op(V, lambda e: e.memset(magic, 1.0), writes=[B_magic])
        P.op(V, lambda e: e.tensor_scalar(out=eps_c, in0=magic, scalar1=float(2.0**-30),
                                          scalar2=1.0, op0=ALU.add, op1=ALU.subtract),
             reads=[B_magic], writes=[B_eps])
        P.op(V, lambda e: e.tensor_scalar(out=magic, in0=eps_c, scalar1=MAGIC_K,
                                          scalar2=MAGIC_LO, op0=ALU.mult, op1=ALU.add),
             reads=[B_eps], writes=[B_magic])
        P.op(V, lambda e: e.memset(ones_r512, 1.0), writes=[B_o512])
        P.op(V, lambda e: e.memset(ones_r128, 1.0), writes=[B_o128])
        P.op(V, lambda e: e.memset(ones_c, 1.0), writes=[B_oc])
        P.dma(G, bq_r, bq_d, writes=[B_bq])
        P.dma(G, bk_r, bk_d, writes=[B_bk])
        P.dma(G, bv_r, bv_d, writes=[B_bv])
        P.dma(G, bo_r, bo_d, writes=[B_bo])
        P.dma(Y, t512_t[0][0:1, 0:1], clipx_d, writes=[B_t512[0]])
        P.dma(Y, t512_t[0][0:1, 1:2], clipd_d, writes=[B_t512[0]])
        P.op(S, lambda e: e.activation(out=t512_t[1][0:1, 0:1], in_=t512_t[0][0:1, 0:1],
                                       func=AF.Sigmoid), reads=[B_t512[0]], writes=[B_t512[1]])
        P.op(S, lambda e: e.activation(out=t512_t[1][0:1, 1:2], in_=t512_t[0][0:1, 1:2],
                                       func=AF.Sigmoid), reads=[B_t512[0]], writes=[B_t512[1]])
        P.dma(Y, scal_d[0:1, 0:1], t512_t[1][0:1, 0:1], reads=[B_t512[1]], writes=[B_scald[0]])
        P.dma(Y, scal_d[1:2, 0:1], t512_t[1][0:1, 1:2], reads=[B_t512[1]], writes=[B_scald[1]])
        P.dma(Y, sigx_c, scal_d[0:1, 0:1].partition_broadcast(128),
              reads=[B_scald[0]], writes=[B_sigx])
        P.dma(Y, sigd_c, scal_d[1:2, 0:1].partition_broadcast(128),
              reads=[B_scald[1]], writes=[B_sigd])

        B_bqc, B_boc = Buf(), Buf()
        bqc_cols = mcol_t[:, 120:128]
        boc_cols = mxb_t[:, 80:88]
        P.dma(G, bqc_cols, bq_d.rearrange("one (a p) -> p (a one)", p=128),
              writes=[B_bqc], slow=True)
        P.dma(G, boc_cols, bo_d.rearrange("one (a p) -> p (a one)", p=128),
              writes=[B_boc], slow=True)

        P.dma(Y, siluc_t[:, 0:2], cT_d, writes=[B_siluc])
        P.op(S, lambda e: e.activation(out=siluc_t[:, 2:4], in_=siluc_t[:, 0:2], func=AF.Silu),
             reads=[B_siluc], writes=[B_siluc])
        bw, bg0 = big_bufs(1)
        B_wmod = bw[0]
        for half in range(2):
            P.dma(Y, big_t[:, 0:3072], wmods_d[:, half * 3072:(half + 1) * 3072],
                  writes=[B_wmod], extra=bg0)
            for j in range(24):
                jg = half * 24 + j
                pb = psb[6 + (jg % 2)]
                pB = B_ps[6 + (jg % 2)]
                P.op(T, lambda e, j=j, pb=pb: e.matmul(
                    out=pb[:, 0:2], lhsT=big_t[:, j * 128:(j + 1) * 128], rhs=siluc_t[:, 2:4],
                    start=True, stop=True),
                    reads=[B_wmod, B_siluc], writes=[pB])
                P.op(V, lambda e, jg=jg, pb=pb: e.tensor_copy(
                    out=modp_t[:, 2 * jg:2 * jg + 2], in_=pb[:, 0:2]),
                    reads=[pB], writes=[B_modp])
        big_close([B_wmod])
        P.dma(G, ar0i, modp_t[:], reads=[B_modp], writes=[B_ar["0i"]])
        P.op(G, lambda e: e.collective_compute(
            "AllReduce", ALU.add, replica_groups=[list(range(NC))], ins=[ar0i], outs=[ar0o]),
            reads=[B_ar["0i"]], writes=[B_ar["0o"]], sem="coll")
        P.dma(G, modc_t[:], ar0o, reads=[B_ar["0o"]], writes=[B_modc])

        def clampg(g0):
            sl = modc_t[:, g0 * 16:g0 * 16 + 16]
            a, b_, c_, d_ = (t1k_t[0][:, k * 16:(k + 1) * 16] for k in range(4))
            P.op(V, lambda e: e.tensor_scalar(out=a, in0=sl, scalar1=-1.0, scalar2=None,
                                              op0=ALU.mult), reads=[B_modc], writes=[B_t1k[0]])
            P.op(V, lambda e: e.tensor_tensor(out=b_, in0=sl, in1=a, op=ALU.max),
                 reads=[B_modc, B_t1k[0]], writes=[B_t1k[0]])
            P.op(V, lambda e: e.tensor_scalar(out=b_, in0=b_, scalar1=0.25, scalar2=None,
                                              op0=ALU.max), reads=[B_t1k[0]], writes=[B_t1k[0]])
            P.op(V, lambda e: e.tensor_scalar(out=c_, in0=sl, scalar1=0.0, scalar2=None,
                                              op0=ALU.is_ge), reads=[B_modc], writes=[B_t1k[0]])
            P.op(V, lambda e: e.tensor_scalar(out=d_, in0=c_, scalar1=2.0, scalar2=-1.0,
                                              op0=ALU.mult, op1=ALU.add),
                 reads=[B_t1k[0]], writes=[B_t1k[0]])
            P.op(V, lambda e: e.tensor_tensor(out=sl, in0=b_, in1=d_, op=ALU.mult),
                 reads=[B_t1k[0]], writes=[B_modc])
        clampg(2)
        clampg(5)
        P.op(V, lambda e: e.tensor_scalar(out=modc_t[:, 16:32], in0=modc_t[:, 16:32],
                                          scalar1=1.0, scalar2=None, op0=ALU.add),
             reads=[B_modc], writes=[B_modc])
        P.op(V, lambda e: e.tensor_scalar(out=modc_t[:, 64:80], in0=modc_t[:, 64:80],
                                          scalar1=1.0, scalar2=None, op0=ALU.add),
             reads=[B_modc], writes=[B_modc])

        for i in range(DC):
            P.dma(G, rbw_t[i][:], wkT_d[i * 128:(i + 1) * 128, :], writes=[B_rbw[i]])
        for i in range(DC):
            P.dma(G, rbw_t[8 + i][:], wvT_d[i * 128:(i + 1) * 128, :], writes=[B_rbw[8 + i]])

        xri = [0]

        def xload(src_d, idx, extra_src=None):
            k = xri[0] % 3
            xri[0] += 1
            ex = [] if extra_src is None else extra_src.rwaits()
            tok = P.dma(Y, xr_t[k][:], src_d[idx * 128:(idx + 1) * 128, :],
                        writes=[B_xr[k]], extra=ex)
            if extra_src is not None:
                extra_src.did_r(tok)
            return xr_t[k][:], B_xr[k]

        def ln_pass(b, src_d, sc0, sc1, Bsc0, Bsc1, src_bufs, drow):
            sum_sl = [psb[0][0:1, 0:512], psb[1][0:1, 0:512]]
            sq_sl = [psb[2][0:1, 0:512], psb[3][0:1, 0:512]]
            scp = [(sc0, sc1, Bsc0, Bsc1), (rbh_t[12][:], rbh_t[13][:], B_rbh[12], B_rbh[13])]
            for c in range(DC):
                s0, s1, Bs0, Bs1 = scp[c % 2]
                buf, Bb = xload(src_d, b * DC + c,
                                src_bufs[b * DC + c] if src_bufs else None)
                P.op(S, lambda e, buf=buf, s=s0: e.activation(out=s[:], in_=buf, func=AF.Copy),
                     reads=[Bb], writes=[Bs0])
                P.op(S, lambda e, buf=buf, s=s1: e.activation(out=s[:], in_=buf, func=AF.Square),
                     reads=[Bb], writes=[Bs1])
                for h in range(2):
                    P.op(T, lambda e, h=h, c=c, s=s0: e.matmul(
                        out=sum_sl[h], lhsT=ones_c, rhs=s[:, h * 512:(h + 1) * 512],
                        start=(c == 0), stop=(c == DC - 1)),
                        reads=[Bs0, B_oc], writes=[B_ps[h]] if c == DC - 1 or c == 0 else [])
                    P.op(T, lambda e, h=h, c=c, s=s1: e.matmul(
                        out=sq_sl[h], lhsT=ones_c, rhs=s[:, h * 512:(h + 1) * 512],
                        start=(c == 0), stop=(c == DC - 1)),
                        reads=[Bs1, B_oc], writes=[B_ps[2 + h]] if c == DC - 1 or c == 0 else [])
            r_M, r_T, r_A, r_B = (row(i) for i in range(4))
            Bm, Bt, BA, BB = (rowbuf(i) for i in range(4))
            P.op(S, lambda e: e.activation(out=r_M[:, 0:1024], in_=psA[0][0:1, 0:1024],
                                           func=AF.Copy),
                 reads=[B_ps[0], B_ps[1]], writes=[Bm])
            P.op(S, lambda e: e.activation(out=r_T[:, 0:1024], in_=psA[1][0:1, 0:1024],
                                           func=AF.Copy),
                 reads=[B_ps[2], B_ps[3]], writes=[Bt])
            P.op(V, lambda e: e.tensor_scalar(out=r_M, in0=r_M, scalar1=1.0 / D,
                                              scalar2=None, op0=ALU.mult),
                 reads=[Bm], writes=[Bm])
            P.op(V, lambda e: e.tensor_tensor(out=r_A, in0=r_M, in1=r_M, op=ALU.mult),
                 reads=[Bm], writes=[BA])
            P.op(V, lambda e: e.scalar_tensor_tensor(out=r_T, in0=r_T, scalar=1.0 / D,
                                                     in1=r_A, op0=ALU.mult, op1=ALU.subtract),
                 reads=[Bt, BA], writes=[Bt])
            P.op(V, lambda e: e.tensor_scalar(out=r_T, in0=r_T, scalar1=1e-5, scalar2=None,
                                              op0=ALU.add), reads=[Bt], writes=[Bt])
            BdA, BdB = rowbuf(100 + drow), rowbuf(101 + drow)
            P.dma(G, rows_d[drow:drow + 1, :], r_T, reads=[Bt], writes=[BdA])
            P.dma(G, rows_d[drow + 1:drow + 2, :], r_M, reads=[Bm], writes=[BdB])

        def ln_bcast(drow):
            BdA, BdB = rowbuf(100 + drow), rowbuf(101 + drow)
            P.dma(G, abA_t[:], rows_d[drow:drow + 1, :].partition_broadcast(128),
                  reads=[BdA], writes=[B_abA])
            P.dma(G, abB_t[:], rows_d[drow + 1:drow + 2, :].partition_broadcast(128),
                  reads=[BdB], writes=[B_abB])
            P.op(S, lambda e: e.activation(out=abA_t[:], in_=abA_t[:], func=AF.Sqrt),
                 reads=[B_abA], writes=[B_abA])
            P.op(V, lambda e: e.reciprocal(out=abA_t[:], in_=abA_t[:]),
                 reads=[B_abA], writes=[B_abA])
            P.op(V, lambda e: e.scalar_tensor_tensor(out=abB_t[:], in0=abB_t[:],
                                                     scalar=-1.0, in1=abA_t[:],
                                                     op0=ALU.mult, op1=ALU.mult),
                 reads=[B_abB, B_abA], writes=[B_abB])

        for b in range(NB):
            ln_pass(b, x_d, bsc_t[0], bsc_t[1], B_bsc[0], B_bsc[1], None, 4 + 2 * b)
        for b in range(NB):
            ln_bcast(4 + 2 * b)
            for c in range(DC):
                buf, Bb = xload(x_d, b * DC + c)
                P.op(S, lambda e, b=b, c=c: e.activation(
                    out=t1k_t[2][:], in_=abB_t[:], func=AF.Identity,
                    scale=mcolv(1, b, c), bias=mcolv(0, b, c)),
                    reads=[B_abB, B_modc], writes=[B_t1k[2]])
                P.op(G, lambda e, buf=buf: e.tensor_tensor(
                    out=t1k_t[1][:], in0=buf, in1=abA_t[:], op=ALU.mult),
                    reads=[Bb, B_abA], writes=[B_t1k[1]])
                P.op(V, lambda e, b=b, c=c: e.scalar_tensor_tensor(
                    out=rbh_t[b * DC + c][:], in0=t1k_t[1][:], scalar=mcolv(1, b, c),
                    in1=t1k_t[2][:], op0=ALU.mult, op1=ALU.add),
                    reads=[B_t1k[1], B_t1k[2], B_modc], writes=[B_rbh[b * DC + c]])

        kv_dma_toks = [[], []]
        B_kvd = [Buf() for _ in range(NB * DC)]
        B_ar1 = [Buf(), Buf()]
        for b in range(NB):
            for n in range(DC):
                p2 = n % 2
                kb = [psb[0 + 4 * p2], psb[1 + 4 * p2]]
                kB = [B_ps[0 + 4 * p2], B_ps[1 + 4 * p2]]
                vb = [psb[2 + 4 * p2], psb[3 + 4 * p2]]
                vB = [B_ps[2 + 4 * p2], B_ps[3 + 4 * p2]]
                kfull = psA[0][:] if p2 == 0 else psA[2][:]
                for i in range(DC):
                    lhs = rbh_t[b * DC + i][:, n * 128:(n + 1) * 128]
                    for h in range(2):
                        P.op(T, lambda e, lhs=lhs, h=h, i=i, kb=kb: e.matmul(
                            out=kb[h][:], lhsT=lhs, rhs=rbw_t[i][:, h * 512:(h + 1) * 512],
                            start=(i == 0), stop=False),
                            reads=[B_rbh[b * DC + i], B_rbw[i]],
                            writes=[kB[h]] if i == 0 else [])
                        P.op(T, lambda e, lhs=lhs, h=h, i=i, vb=vb: e.matmul(
                            out=vb[h][:], lhsT=lhs, rhs=rbw_t[8 + i][:, h * 512:(h + 1) * 512],
                            start=(i == 0), stop=False),
                            reads=[B_rbh[b * DC + i], B_rbw[8 + i]],
                            writes=[vB[h]] if i == 0 else [])
                for h in range(2):
                    P.op(T, lambda e, h=h, kb=kb: e.matmul(
                        out=kb[h][:], lhsT=ones_r128, rhs=bk_r[0:1, h * 512:(h + 1) * 512],
                        start=False, stop=True),
                        reads=[B_o128, B_bk], writes=[kB[h]])
                    P.op(T, lambda e, h=h, vb=vb: e.matmul(
                        out=vb[h][:], lhsT=ones_r128, rhs=bv_r[0:1, h * 512:(h + 1) * 512],
                        start=False, stop=True),
                        reads=[B_o128, B_bv], writes=[vB[h]])
                km_, Bkm = (t1k_t[0], B_t1k[0]) if p2 == 0 else (t1k_t[2], B_t1k[2])
                ke_, Bke = (t1k_t[1], B_t1k[1]) if p2 == 0 else (rz_t, B_rz)
                P.op(V, lambda e, kfull=kfull, km_=km_: e.tensor_scalar(
                    out=km_[:], in0=kfull, scalar1=0.0, scalar2=None, op0=ALU.min),
                    reads=kB, writes=[Bkm])
                P.op(S, lambda e, km_=km_, ke_=ke_: e.activation(
                    out=ke_[:], in_=km_[:], func=AF.Exp),
                    reads=[Bkm], writes=[Bke])
                P.op(V, lambda e, n=n, kfull=kfull, ke_=ke_: e.scalar_tensor_tensor(
                    out=rbk_t[n][:], in0=kfull, scalar=0.0,
                    in1=ke_[:], op0=ALU.max, op1=ALU.add),
                    reads=kB + [Bke], writes=[B_rbk[n]])
                if p2 == 0:
                    P.op(S, lambda e, n=n: e.activation(
                        out=rbk_t[8 + n][:], in_=psA[1][:],
                        func=AF.Copy), reads=vB, writes=[B_rbk[8 + n]])
                else:
                    for h in range(2):
                        P.op(S, lambda e, n=n, h=h, vb=vb: e.activation(
                            out=rbk_t[8 + n][:, h * 512:(h + 1) * 512], in_=vb[h][:],
                            func=AF.Copy), reads=[vB[h]], writes=[B_rbk[8 + n]])
            for dc in range(DC):
                p2 = dc % 2
                wb = [psb[4 - 4 * p2], psb[5 - 4 * p2]]
                wB = [B_ps[4 - 4 * p2], B_ps[5 - 4 * p2]]
                wfull = psA[2][:] if p2 == 0 else psA[0][:]
                ksb = psb[6 + p2]
                ksB = B_ps[6 + p2]
                for n in range(DC):
                    lhs = rbk_t[n][:, dc * 128:(dc + 1) * 128]
                    for h in range(2):
                        P.op(T, lambda e, lhs=lhs, n=n, h=h, wb=wb: e.matmul(
                            out=wb[h][:], lhsT=lhs, rhs=rbk_t[8 + n][:, h * 512:(h + 1) * 512],
                            start=(n == 0), stop=(n == DC - 1)),
                            reads=[B_rbk[n], B_rbk[8 + n]],
                            writes=[wB[h]] if n == 0 or n == DC - 1 else [])
                    P.op(T, lambda e, lhs=lhs, n=n, ksb=ksb: e.matmul(
                        out=ksb[:, 0:1], lhsT=lhs, rhs=ones_c,
                        start=(n == 0), stop=(n == DC - 1)),
                        reads=[B_rbk[n], B_oc],
                        writes=[ksB] if n == 0 or n == DC - 1 else [])
                kvb_ = kvt_t[dc % 2][:]
                Bk = B_kvt[dc % 2]
                P.op(V, lambda e, kvb_=kvb_, wfull=wfull: e.tensor_copy(
                    out=kvb_[0:128, 0:1024], in_=wfull),
                     reads=wB, writes=[Bk])
                P.op(V, lambda e, kvb_=kvb_, ksb=ksb: e.tensor_copy(
                    out=kvb_[0:128, 1024:1025], in_=ksb[:, 0:1]),
                     reads=[ksB], writes=[Bk])
                kv_dma_toks[b].append(P.dma(
                    Y, kvari[(b * DC + dc) * 128:(b * DC + dc + 1) * 128, :], kvb_,
                    reads=[Bk], writes=[B_kvd[b * DC + dc]]))
            P.op(G, lambda e, b=b: e.collective_compute(
                "AllReduce", ALU.add, replica_groups=[list(range(NC))],
                ins=[kvari[b * DC * 128:(b + 1) * DC * 128, :]],
                outs=[kvaro[b * DC * 128:(b + 1) * DC * 128, :]]),
                reads=B_kvd[b * DC:(b + 1) * DC], writes=[B_ar1[b]], sem="coll",
                extra=kv_dma_toks[b])

        for i in range(DC):
            P.dma(G, rbw_t[i][:], wqT_d[i * 128:(i + 1) * 128, :], writes=[B_rbw[i]])
        for b in range(NB):
            for oc in range(DC):
                pq = [psb[0 + 2 * (oc % 2)], psb[1 + 2 * (oc % 2)]]
                pB = [B_ps[0 + 2 * (oc % 2)], B_ps[1 + 2 * (oc % 2)]]
                for i in range(DC):
                    for h in range(2):
                        P.op(T, lambda e, i=i, oc=oc, h=h, b=b, pq=pq: e.matmul(
                            out=pq[h][:], lhsT=rbw_t[i][:, oc * 128:(oc + 1) * 128],
                            rhs=rbh_t[b * DC + i][:, h * 512:(h + 1) * 512],
                            start=(i == 0), stop=(i == DC - 1)),
                            reads=[B_rbw[i], B_rbh[b * DC + i]],
                            writes=[pB[h]] if i == 0 or i == DC - 1 else [])
                pqf = psA[oc % 2][:]
                ts_, Bts = (t1k_t[0], B_t1k[0]) if oc % 2 == 0 else (t1k_t[2], B_t1k[2])
                te_, Bte = (t1k_t[1], B_t1k[1]) if oc % 2 == 0 else (rz_t, B_rz)
                bqc = bqc_cols[:, oc:oc + 1]
                P.op(V, lambda e, pqf=pqf, ts_=ts_, bqc=bqc: e.tensor_scalar(
                    out=ts_[:], in0=pqf, scalar1=bqc, scalar2=None, op0=ALU.add),
                    reads=pB + [B_bqc], writes=[Bts])
                P.op(V, lambda e, ts_=ts_, te_=te_: e.tensor_scalar(
                    out=te_[:], in0=ts_[:], scalar1=0.0, scalar2=None, op0=ALU.min),
                    reads=[Bts], writes=[Bte])
                P.op(S, lambda e, te_=te_: e.activation(
                    out=te_[:], in_=te_[:], func=AF.Exp),
                    reads=[Bte], writes=[Bte])
                P.op(V, lambda e, b=b, oc=oc, ts_=ts_, te_=te_: e.scalar_tensor_tensor(
                    out=rbk_t[b * DC + oc][:], in0=ts_[:],
                    scalar=0.0, in1=te_[:], op0=ALU.max, op1=ALU.add),
                    reads=[Bts, Bte], writes=[B_rbk[b * DC + oc]])

        for i in range(DC):
            P.dma(G, rbw_t[8 + i][:], woT_d[i * 128:(i + 1) * 128, :], writes=[B_rbw[8 + i]])
        for b in range(NB):
            kvt_set = rbh_t[0:8] if b == 0 else rbw_t[0:8]
            B_kvt_set = B_rbh[0:8] if b == 0 else B_rbw[0:8]
            kvld = []
            for dc in range(DC):
                t = P.dma(G, kvt_set[dc][:],
                          kvaro[(b * DC + dc) * 128:(b * DC + dc + 1) * 128, 0:1024],
                          reads=[B_ar1[b]], writes=[B_kvt_set[dc]])
                kvld.append(t)
            ks8 = mcol_t[:, 112:120]
            P.dma(G, ks8,
                  kvaro[b * DC * 128:(b + 1) * DC * 128, 1024:1025]
                  .rearrange("(a p) one -> p (a one)", p=128),
                  reads=[B_ar1[b]], writes=[B_kst], slow=True)
            P.op(V, lambda e, ks8=ks8: e.tensor_scalar(
                out=ks8, in0=ks8, scalar1=1e-6, scalar2=None, op0=ALU.add),
                reads=[B_kst], writes=[B_kst])
            P.op(V, lambda e, ks8=ks8: e.tensor_copy(out=rbx_t[:, 4750:4758], in_=ks8),
                 reads=[B_kst], writes=B_ks)
            zsl = [psb[4][0:1, 0:512], psb[5][0:1, 0:512]]
            for dc in range(DC):
                for h in range(2):
                    P.op(T, lambda e, dc=dc, h=h, b=b, zsl=zsl: e.matmul(
                        out=zsl[h], lhsT=ks_cs[dc], rhs=rbk_t[b * DC + dc][:, h * 512:(h + 1) * 512],
                        start=(dc == 0), stop=(dc == DC - 1)),
                        reads=[B_ks[dc], B_rbk[b * DC + dc]],
                        writes=[B_ps[4 + h]] if dc == 0 or dc == DC - 1 else [])
            Bz8, Bz9 = rowbuf(0), rowbuf(1)
            P.op(S, lambda e: e.activation(out=row(0)[:, 0:1024], in_=psA[2][0:1, 0:1024],
                                           func=AF.Copy),
                 reads=[B_ps[4], B_ps[5]], writes=[Bz8])
            P.op(V, lambda e: e.reciprocal(out=row(1), in_=row(0)), reads=[Bz8], writes=[Bz9])
            Bz9d = rowbuf(109)
            P.dma(Y, rows_d[9:10, :], row(1), reads=[Bz9], writes=[Bz9d])
            P.dma(Y, rz_t[:], rows_d[9:10, :].partition_broadcast(128),
                  reads=[Bz9d], writes=[B_rz])
            for ec in range(DC):
                pc = [psb[0 + 2 * (ec % 2)], psb[1 + 2 * (ec % 2)]]
                pB = [B_ps[0 + 2 * (ec % 2)], B_ps[1 + 2 * (ec % 2)]]
                for dc in range(DC):
                    for h in range(2):
                        P.op(T, lambda e, dc=dc, ec=ec, h=h, b=b, pc=pc,
                             kv=kvt_set: e.matmul(
                            out=pc[h][:], lhsT=kv[dc][:, ec * 128:(ec + 1) * 128],
                            rhs=rbk_t[b * DC + dc][:, h * 512:(h + 1) * 512],
                            start=(dc == 0), stop=(dc == DC - 1)),
                            reads=[B_kvt_set[dc], B_rbk[b * DC + dc]],
                            writes=[pB[h]] if dc == 0 or dc == DC - 1 else [])
                P.op(V, lambda e, ec=ec: e.tensor_tensor(
                    out=rbh_t[8 + ec][:], in0=psA[ec % 2][:],
                    in1=rz_t[:], op=ALU.mult),
                    reads=pB + [B_rz], writes=[B_rbh[8 + ec]])
            for oc in range(DC):
                po = [psb[0 + 2 * (oc % 2)], psb[1 + 2 * (oc % 2)]]
                pB = [B_ps[0 + 2 * (oc % 2)], B_ps[1 + 2 * (oc % 2)]]
                for ec in range(DC):
                    for h in range(2):
                        P.op(T, lambda e, ec=ec, oc=oc, h=h, po=po: e.matmul(
                            out=po[h][:], lhsT=rbw_t[8 + ec][:, oc * 128:(oc + 1) * 128],
                            rhs=rbh_t[8 + ec][:, h * 512:(h + 1) * 512],
                            start=(ec == 0), stop=(ec == DC - 1)),
                            reads=[B_rbw[8 + ec], B_rbh[8 + ec]],
                            writes=[pB[h]] if ec == 0 or ec == DC - 1 else [])
                buf, Bb = xload(x_d, b * DC + oc)
                x2t = t1k_t[2][:]
                P.op(V, lambda e, b=b, oc=oc, x2t=x2t: e.tensor_scalar(
                    out=x2t, in0=psA[oc % 2][:], scalar1=boc_cols[:, oc:oc + 1],
                    scalar2=mcolv(2, b, oc), op0=ALU.add, op1=ALU.mult),
                    reads=pB + [B_boc, B_modc], writes=[B_t1k[2]])
                P.op(V, lambda e, x2t=x2t, buf=buf: e.tensor_tensor(
                    out=x2t, in0=x2t, in1=buf[0:128, 0:1024], op=ALU.add),
                    reads=[B_t1k[2], Bb], writes=[B_t1k[2]])
                P.dma(Y, x2sp[(b * DC + oc) * 128:(b * DC + oc + 1) * 128, :], x2t,
                      reads=[B_t1k[2]], writes=[B_x2d[b * DC + oc]])

        nmx = 0
        for b in range(NB):
            ln_pass(b, x2sp, bsc_t[0], bsc_t[1], B_bsc[0], B_bsc[1], B_x2d, 4 + 2 * b)
        for b in range(NB):
            ln_bcast(4 + 2 * b)
            for c in range(DC):
                buf, Bb = xload(x2sp, b * DC + c, B_x2d[b * DC + c])
                P.op(S, lambda e, b=b, c=c: e.activation(
                    out=t1k_t[2][:], in_=abB_t[:], func=AF.Identity,
                    scale=mcolv(4, b, c), bias=mcolv(3, b, c)),
                    reads=[B_abB, B_modc], writes=[B_t1k[2]])
                P.op(G, lambda e, buf=buf: e.tensor_tensor(
                    out=t1k_t[1][:], in0=buf, in1=abA_t[:], op=ALU.mult),
                    reads=[Bb, B_abA], writes=[B_t1k[1]])
                P.op(V, lambda e, b=b, c=c: e.scalar_tensor_tensor(
                    out=rbk_t[b * DC + c][:], in0=t1k_t[1][:], scalar=mcolv(4, b, c),
                    in1=t1k_t[2][:], op0=ALU.mult, op1=ALU.add),
                    reads=[B_t1k[1], B_t1k[2], B_modc], writes=[B_rbk[b * DC + c]])
                P.op(V, lambda e, b=b, c=c, nmx=nmx: e.tensor_reduce(
                    out=mxb_t[:, nmx:nmx + 1], in_=rbk_t[b * DC + c][:], axis=AX.X,
                    op=ALU.max, apply_absolute_value=True),
                    reads=[B_rbk[b * DC + c]], writes=[B_mxb])
                nmx += 1
        bufs6, bg6 = big_bufs(4)
        Bfg = bufs6[0:2]
        Bfu = bufs6[2:4]
        fg = [big_t[:, 0:1024], big_t[:, 1024:2048]]
        fu = [big_t[:, 2048:3072], big_t[:, 3072:4096]]
        Bsg = [B_rbk[0], B_rbk[1]]
        Bsu = [B_rbk[2], B_rbk[3]]
        Bsi = [B_rbk[4], B_rbk[5], B_rbk[6], B_rbk[7]]
        B_mg = [Buf(), Buf()]
        B_mu = [Buf(), Buf()]
        B_rg6 = [Buf(), Buf()]
        B_ru6 = [Buf(), Buf()]
        B_sg2 = [Buf(), Buf()]
        B_su2 = [Buf(), Buf()]
        sidx = [0]
        first6 = [bg6]

        def load6(ic):
            p6 = ic % 2
            P.dma(Y, fg[p6], wg_d[ic * 128:(ic + 1) * 128, :], writes=[Bfg[p6]],
                  extra=first6[0])
            P.dma(G, fu[p6], wu_d[ic * 128:(ic + 1) * 128, :], writes=[Bfu[p6]],
                  extra=first6[0])
            first6[0] = []

        def cols6(p6):
            return (mcol_t[:, 0 + p6:1 + p6], mcol_t[:, 2 + p6:3 + p6],
                    mcol_t[:, 4 + p6:5 + p6], mcol_t[:, 6 + p6:7 + p6],
                    mcol_t[:, 8 + p6:9 + p6], mcol_t[:, 10 + p6:11 + p6])

        QG = [bsc_t[0], rbk_t[4]]
        QU = [bsc_t[1], rbk_t[5]]
        BQG = [B_bsc[0], B_rbk[4]]
        BQU = [B_bsc[1], B_rbk[5]]

        def prep6q(ic):
            p6 = ic % 2
            fgb, fub = fg[p6], fu[p6]
            Bg, Bu = Bfg[p6], Bfu[p6]
            mgc, muc, rgc, ruc, sgc, suc = cols6(p6)
            for (fb, Bf, mc, Bm, rc, Br) in (
                    (fgb, Bg, mgc, B_mg[p6], rgc, B_rg6[p6]),
                    (fub, Bu, muc, B_mu[p6], ruc, B_ru6[p6])):
                P.op(V, lambda e, fb=fb, mc=mc: e.tensor_reduce(
                    out=mc, in_=fb, axis=AX.X, op=ALU.max, apply_absolute_value=True),
                    reads=[Bf], writes=[Bm])
                P.op(V, lambda e, mc=mc, rc=rc: e.reciprocal(out=rc, in_=mc),
                     reads=[Bm], writes=[Br])
                P.op(V, lambda e, fb=fb, rc=rc: e.tensor_scalar(
                    out=fb, in0=fb, scalar1=rc, scalar2=127.0, op0=ALU.mult, op1=ALU.mult),
                    reads=[Bf, Br], writes=[Bf])
                P.op(V, lambda e, fb=fb: e.tensor_scalar(
                    out=fb, in0=fb, scalar1=magic, scalar2=magic, op0=ALU.add,
                    op1=ALU.subtract), reads=[Bf, B_magic], writes=[Bf])

        def prep6t(ic):
            p6 = ic % 2
            fgb, fub = fg[p6], fu[p6]
            Bg, Bu = Bfg[p6], Bfu[p6]
            qg, qu = QG[p6], QU[p6]
            Bqg, Bqu = BQG[p6], BQU[p6]
            for hc in range(DC):
                for (fb, Bf, qt, Bq, pk) in ((fgb, Bg, qg, Bqg, 0), (fub, Bu, qu, Bqu, 1)):
                    pt = psT[pk]
                    pBt = B_ps[6 + pk]
                    P.op(T, lambda e, fb=fb, hc=hc, pt=pt: e.transpose(
                        out=pt[:, (hc % 4) * 128:(hc % 4 + 1) * 128],
                        in_=fb[:, hc * 128:(hc + 1) * 128],
                        identity=ident_t[:]), reads=[Bf, B_ident], writes=[pBt])
                    if hc % 4 == 3:
                        hg = hc // 4
                        P.op(S, lambda e, hg=hg, qt=qt, pt=pt: e.activation(
                            out=qt[:, hg * 512:(hg + 1) * 512], in_=pt[:, 0:512],
                            func=AF.Copy), reads=[pBt], writes=[Bq])

        def comp6(ic):
            p6 = ic % 2
            mgc, muc, rgc, ruc, sgc, suc = cols6(p6)
            qg, qu = QG[p6], QU[p6]
            Bqg, Bqu = BQG[p6], BQU[p6]
            P.op(V, lambda e, mgc=mgc, sgc=sgc: e.tensor_scalar(
                out=sgc, in0=mgc, scalar1=s2_c, scalar2=1.0 / 127.0, op0=ALU.mult,
                op1=ALU.mult), reads=[B_mg[p6], B_s2], writes=[B_sg2[p6]])
            P.op(V, lambda e, muc=muc, suc=suc: e.tensor_scalar(
                out=suc, in0=muc, scalar1=s2_c, scalar2=1.0 / 127.0, op0=ALU.mult,
                op1=ALU.mult), reads=[B_mu[p6], B_s2], writes=[B_su2[p6]])
            for b in range(NB):
                gA, uA = (psA[0], psA[1]) if b == 0 else (psA[2], psA[0])
                gb0, ub0 = (0, 2) if b == 0 else (4, 0)
                pBg = [B_ps[gb0], B_ps[gb0 + 1]]
                pBu = [B_ps[ub0], B_ps[ub0 + 1]]
                for hc in range(DC):
                    for h in range(2):
                        P.op(T, lambda e, hc=hc, h=h, b=b, qg=qg, gA=gA: e.matmul(
                            out=gA[:, h * 512:(h + 1) * 512],
                            lhsT=qg[:, hc * 128:(hc + 1) * 128],
                            rhs=rbh_t[b * DC + hc][:, h * 512:(h + 1) * 512],
                            start=(hc == 0), stop=(hc == DC - 1)),
                            reads=[Bqg, B_rbh[b * DC + hc]],
                            writes=[pBg[h]] if hc == 0 or hc == DC - 1 else [])
                        P.op(T, lambda e, hc=hc, h=h, b=b, qu=qu, uA=uA: e.matmul(
                            out=uA[:, h * 512:(h + 1) * 512],
                            lhsT=qu[:, hc * 128:(hc + 1) * 128],
                            rhs=rbh_t[b * DC + hc][:, h * 512:(h + 1) * 512],
                            start=(hc == 0), stop=(hc == DC - 1)),
                            reads=[Bqu, B_rbh[b * DC + hc]],
                            writes=[pBu[h]] if hc == 0 or hc == DC - 1 else [])
                t1x = t1k_t[sidx[0] % 2]
                Bt1x = B_t1k[sidx[0] % 2]
                P.op(S, lambda e, sgc=sgc, t1x=t1x, gA=gA: e.activation(
                    out=t1x[:], in_=gA[:], func=AF.Silu, scale=sgc),
                    reads=pBg + [B_sg2[p6]], writes=[Bt1x])
                P.op(V, lambda e, suc=suc, t1x=t1x, uA=uA: e.scalar_tensor_tensor(
                    out=t1x[:], in0=uA[:], scalar=suc, in1=t1x[:],
                    op0=ALU.mult, op1=ALU.mult),
                    reads=pBu + [B_su2[p6], Bt1x], writes=[Bt1x])
                P.op(V, lambda e, k=24 + sidx[0], t1x=t1x: e.tensor_reduce(
                    out=mxb_t[:, k:k + 1], in_=t1x[:], axis=AX.X, op=ALU.max,
                    apply_absolute_value=True), reads=[Bt1x], writes=[B_mxb])
                P.dma(G, sisp[(b * IC + ic) * 128:(b * IC + ic + 1) * 128, :], t1x[:],
                      reads=[Bt1x], writes=[B_sid[b * IC + ic]])
                sidx[0] += 1

        load6(0)
        load6(1)
        prep6q(0)
        prep6t(0)
        P.op(V, lambda e, n=nmx: e.tensor_reduce(
            out=mxb_t[:, 20:21], in_=mxb_t[:, 0:n], axis=AX.X, op=ALU.max),
            reads=[B_mxb], writes=[B_mxb])
        P.op(T, lambda e: e.transpose(out=psb[5][0:1, 0:128], in_=mxb_t[:, 20:21],
                                      identity=ident_t[:]),
             reads=[B_mxb, B_ident], writes=[B_ps[5]])
        Br10 = rowbuf(10)
        r10 = rows_t[0:1, 4 * TB:4 * TB + 1]
        P.op(V, lambda e: e.tensor_reduce(out=r10, in_=psb[5][0:1, 0:128],
                                          axis=AX.X, op=ALU.max),
             reads=[B_ps[5]], writes=[Br10])
        P.dma(Y, ar2i, r10, reads=[Br10], writes=[B_ar["2i"]])
        P.op(G, lambda e: e.collective_compute(
            "AllReduce", ALU.max, replica_groups=[list(range(NC))], ins=[ar2i], outs=[ar2o]),
            reads=[B_ar["2i"]], writes=[B_ar["2o"]], sem="coll")
        P.dma(G, tsc_c, ar2o.partition_broadcast(128), reads=[B_ar["2o"]], writes=[B_tsc])
        P.op(V, lambda e: e.tensor_tensor(out=tsc_c, in0=tsc_c, in1=sigx_c, op=ALU.mult),
             reads=[B_tsc, B_sigx], writes=[B_tsc])
        P.op(V, lambda e: e.tensor_scalar(out=s2_c, in0=tsc_c, scalar1=1.0 / 127.0,
                                          scalar2=None, op0=ALU.mult),
             reads=[B_tsc], writes=[B_s2])
        P.op(V, lambda e: e.reciprocal(out=rs2_c, in_=tsc_c), reads=[B_tsc], writes=[B_rs2])
        P.op(V, lambda e: e.tensor_scalar(out=rs2_c, in0=rs2_c, scalar1=127.0, scalar2=None,
                                          op0=ALU.mult), reads=[B_rs2], writes=[B_rs2])
        for b in range(NB):
            for c in range(DC):
                P.op(S, lambda e, b=b, c=c: e.activation(
                    out=t1k_t[2][:], in_=rbk_t[b * DC + c][:], func=AF.Identity,
                    bias=magic, scale=rs2_c),
                    reads=[B_rbk[b * DC + c], B_rs2, B_magic], writes=[B_t1k[2]])
                P.op(V, lambda e: e.tensor_scalar(
                    out=t1k_t[2][:], in0=t1k_t[2][:], scalar1=magic, scalar2=127.0,
                    op0=ALU.subtract, op1=ALU.min),
                    reads=[B_t1k[2], B_magic], writes=[B_t1k[2]])
                P.op(V, lambda e, b=b, c=c: e.tensor_scalar(
                    out=rbh_t[b * DC + c][:], in0=t1k_t[2][:], scalar1=-128.0, scalar2=None,
                    op0=ALU.max), reads=[B_t1k[2]], writes=[B_rbh[b * DC + c]])

        wdq_t = [rbw_t[i] for i in range(16)] + [rbk_t[8 + i] for i in range(6)]
        B_wdq = [B_rbw[i] for i in range(16)] + [B_rbk[8 + i] for i in range(6)]
        mdcols = mcol_t[:, 96:104]
        B_md = Buf()
        rd7_c, B_rd7 = col("rd7")
        md_sc = mcol_t[:, 104:107]
        B_mdsc = Buf()
        wsub = [(t1k_t[0][:, 0:1024], B_t1k[0], 1024),
                (t1k_t[1][:, 0:1024], B_t1k[1], 1024),
                (t1k_t[2][:, 0:768], B_t1k[2], 768)]
        for j in range(DC):
            for si3, (tt, Bt3, w) in enumerate(wsub):
                P.dma(Y if si3 % 2 == 0 else G, tt,
                      wdn_d[j * 128:(j + 1) * 128, si3 * 1024:si3 * 1024 + w],
                      writes=[Bt3])
                P.op(V, lambda e, tt=tt, si3=si3: e.tensor_reduce(
                    out=md_sc[:, si3:si3 + 1], in_=tt, axis=AX.X, op=ALU.max,
                    apply_absolute_value=True), reads=[Bt3], writes=[B_mdsc])
            mdc = mdcols[:, j:j + 1]
            P.op(V, lambda e, mdc=mdc: e.tensor_tensor(
                out=mdc, in0=md_sc[:, 0:1], in1=md_sc[:, 1:2], op=ALU.max),
                reads=[B_mdsc], writes=[B_md])
            P.op(V, lambda e, mdc=mdc: e.tensor_tensor(
                out=mdc, in0=mdc, in1=md_sc[:, 2:3], op=ALU.max),
                reads=[B_md, B_mdsc], writes=[B_md])
            P.op(V, lambda e, mdc=mdc: e.reciprocal(out=rd7_c, in_=mdc),
                 reads=[B_md], writes=[B_rd7])
            for (tt, Bt3, w) in wsub:
                P.op(V, lambda e, tt=tt: e.tensor_scalar(out=tt, in0=tt, scalar1=rd7_c,
                                                         scalar2=127.0, op0=ALU.mult,
                                                         op1=ALU.mult),
                     reads=[Bt3, B_rd7], writes=[Bt3])
                P.op(V, lambda e, tt=tt: e.tensor_scalar(out=tt, in0=tt, scalar1=magic,
                                                         scalar2=magic, op0=ALU.add,
                                                         op1=ALU.subtract),
                     reads=[Bt3, B_magic], writes=[Bt3])
            for i in range(IC):
                tt, Bt3, _w = wsub[i // 8]
                pt = psb[6 + i % 2]
                pBt = B_ps[6 + i % 2]
                P.op(T, lambda e, tt=tt, i=i, pt=pt: e.transpose(
                    out=pt[:, 0:128], in_=tt[:, (i % 8) * 128:(i % 8 + 1) * 128],
                    identity=ident_t[:]), reads=[Bt3, B_ident], writes=[pBt])
                P.op(S, lambda e, i=i, j=j, pt=pt: e.activation(
                    out=wdq_t[i][:, j * 128:(j + 1) * 128], in_=pt[:, 0:128],
                    func=AF.Copy), reads=[pBt], writes=[B_wdq[i]])

        for ic in range(IC):
            if ic + 2 < IC:
                load6(ic + 2)
            if ic + 1 < IC:
                prep6q(ic + 1)
            comp6(ic)
            if ic + 1 < IC:
                prep6t(ic + 1)
        big_close(bufs6)
        P.op(V, lambda e, n=24 + sidx[0]: e.tensor_reduce(
            out=mxb_t[:, 21:22], in_=mxb_t[:, 24:n], axis=AX.X, op=ALU.max),
            reads=[B_mxb], writes=[B_mxb])
        P.op(T, lambda e: e.transpose(out=psb[5][0:1, 0:128], in_=mxb_t[:, 21:22],
                                      identity=ident_t[:]),
             reads=[B_mxb, B_ident], writes=[B_ps[5]])
        Br11 = rowbuf(11)
        r11 = rows_t[0:1, 4 * TB + 1:4 * TB + 2]
        P.op(V, lambda e: e.tensor_reduce(out=r11, in_=psb[5][0:1, 0:128],
                                          axis=AX.X, op=ALU.max),
             reads=[B_ps[5]], writes=[Br11])
        P.dma(Y, ar3i, r11, reads=[Br11], writes=[B_ar["3i"]])
        P.op(G, lambda e: e.collective_compute(
            "AllReduce", ALU.max, replica_groups=[list(range(NC))], ins=[ar3i], outs=[ar3o]),
            reads=[B_ar["3i"]], writes=[B_ar["3o"]], sem="coll")
        P.dma(G, tsc_c, ar3o.partition_broadcast(128), reads=[B_ar["3o"]], writes=[B_tsc])
        P.op(V, lambda e: e.tensor_tensor(out=tsc_c, in0=tsc_c, in1=sigd_c, op=ALU.mult),
             reads=[B_tsc, B_sigd], writes=[B_tsc])
        P.op(V, lambda e: e.tensor_scalar(out=s3_c, in0=tsc_c, scalar1=1.0 / 127.0,
                                          scalar2=None, op0=ALU.mult),
             reads=[B_tsc], writes=[B_s3])
        P.op(V, lambda e: e.reciprocal(out=rs3_c, in_=tsc_c), reads=[B_tsc], writes=[B_rs3])
        P.op(V, lambda e: e.tensor_scalar(out=rs3_c, in0=rs3_c, scalar1=127.0, scalar2=None,
                                          op0=ALU.mult), reads=[B_rs3], writes=[B_rs3])
        P.op(V, lambda e: e.tensor_scalar(out=s3_c, in0=s3_c, scalar1=1.0 / 127.0,
                                          scalar2=None, op0=ALU.mult),
             reads=[B_s3], writes=[B_s3])
        for b in range(NB):
            for hc in range(DC):
                mxc = mxb_t[:, 64 + b * DC + hc:65 + b * DC + hc]
                P.op(V, lambda e, b=b, hc=hc, mxc=mxc: e.scalar_tensor_tensor(
                    out=mxc, in0=mdcols[:, hc:hc + 1], scalar=s3_c, in1=mcolv(5, b, hc),
                    op0=ALU.mult, op1=ALU.mult),
                    reads=[B_md, B_s3, B_modc], writes=[B_mxb])

        B_siq = [B_rbh[i] for i in range(11)]

        def siq_ap(i):
            return rbh_t[i % 11][0:128, (i // 11) * 512:(i // 11 + 1) * 512]

        for b in range(NB):
            for th in range(2):
                x2buf = {}
                for i in range(IC):
                    P.dma(Y, siq_ap(i),
                          sisp[(b * IC + i) * 128:(b * IC + i + 1) * 128,
                               th * 512:(th + 1) * 512],
                          reads=[B_sid[b * IC + i]], writes=[B_siq[i % 11]])
                    if i >= IC - 3:
                        hcp = i - (IC - 3)
                        k7 = xri[0] % 3
                        xri[0] += 1
                        t7 = P.dma(Y, xr_t[k7][0:128, 0:512],
                                   x2sp[(b * DC + hcp) * 128:(b * DC + hcp + 1) * 128,
                                        th * 512:(th + 1) * 512],
                                   writes=[B_xr[k7]],
                                   extra=B_x2d[b * DC + hcp].rwaits())
                        B_x2d[b * DC + hcp].did_r(t7)
                        x2buf[hcp] = (xr_t[k7], B_xr[k7])
                    P.op(S, lambda e, i=i: e.activation(
                        out=t512_t[i % 2][:], in_=siq_ap(i), func=AF.Identity, bias=magic,
                        scale=rs3_c), reads=[B_siq[i % 11], B_rs3, B_magic],
                        writes=[B_t512[i % 2]])
                    P.op(V, lambda e, i=i: e.tensor_scalar(
                        out=t512_t[2 + i % 2][:], in0=t512_t[i % 2][:], scalar1=magic,
                        scalar2=127.0, op0=ALU.subtract, op1=ALU.min),
                        reads=[B_t512[i % 2], B_magic], writes=[B_t512[2 + i % 2]])
                    P.op(V, lambda e, i=i: e.tensor_scalar(
                        out=siq_ap(i), in0=t512_t[2 + i % 2][:], scalar1=-128.0, scalar2=None,
                        op0=ALU.max), reads=[B_t512[2 + i % 2]], writes=[B_siq[i % 11]])
                    for hc in range(DC):
                        P.op(T, lambda e, i=i, hc=hc: e.matmul(
                            out=psb[hc][:], lhsT=wdq_t[i][0:128, hc * 128:(hc + 1) * 128],
                            rhs=siq_ap(i), start=(i == 0), stop=(i == IC - 1)),
                            reads=[B_wdq[i], B_siq[i % 11]],
                            writes=[B_ps[hc]] if i == 0 or i == IC - 1 else [])
                for hc in range(DC):
                    pd = psb[hc]
                    pB = B_ps[hc]
                    if hc in x2buf:
                        buf, Bb = x2buf[hc]
                    else:
                        k7 = xri[0] % 3
                        xri[0] += 1
                        buf, Bb = xr_t[k7], B_xr[k7]
                        t7 = P.dma(Y, buf[0:128, 0:512],
                                   x2sp[(b * DC + hc) * 128:(b * DC + hc + 1) * 128,
                                        th * 512:(th + 1) * 512],
                                   writes=[Bb], extra=B_x2d[b * DC + hc].rwaits())
                        B_x2d[b * DC + hc].did_r(t7)
                    P.op(V, lambda e, b=b, hc=hc, buf=buf, pd=pd: e.scalar_tensor_tensor(
                        out=t512_t[4 + hc % 2][:], in0=pd[:],
                        scalar=mxb_t[:, 64 + b * DC + hc:65 + b * DC + hc],
                        in1=buf[0:128, 0:512], op0=ALU.mult, op1=ALU.add),
                        reads=[pB, Bb, B_mxb], writes=[B_t512[4 + hc % 2]])
                    P.dma(G, out_d[(b * DC + hc) * 128:(b * DC + hc + 1) * 128,
                                   th * 512:(th + 1) * 512],
                          t512_t[4 + hc % 2][:], reads=[B_t512[4 + hc % 2]])
        P.barrier()

        @block.sync
        def _(e):
            for f in P.qs["sync"]:
                f(e)

        @block.vector
        def _(e):
            for f in P.qs["vector"]:
                f(e)

        @block.scalar
        def _(e):
            for f in P.qs["scalar"]:
                f(e)

        @block.tensor
        def _(e):
            for f in P.qs["tensor"]:
                f(e)

        @block.gpsimd
        def _(e):
            for f in P.qs["gpsimd"]:
                f(e)

    return nc


_NC_CACHE = [None]


def kernel(L0_attn_x, L0_attn_Wq, L0_attn_bq, L0_attn_Wk, L0_attn_bk, L0_attn_Wv,
           L0_attn_bv, L0_attn_Wo, L0_attn_bo, L0_ffn_clip_x, L0_ffn_clip_down,
           L0_ffn_w_gate, L0_ffn_w_up, L0_ffn_w_down, L0_c, L0_wmod):
    cc = np.ascontiguousarray
    x = np.asarray(L0_attn_x, dtype=np.float32)
    xfm = cc(x.transpose(0, 2, 1))
    wqT = cc(np.asarray(L0_attn_Wq, np.float32).T)
    wkT = cc(np.asarray(L0_attn_Wk, np.float32).T)
    wvT = cc(np.asarray(L0_attn_Wv, np.float32).T)
    woT = cc(np.asarray(L0_attn_Wo, np.float32).T)
    bq = cc(np.asarray(L0_attn_bq, np.float32).reshape(1, D))
    bk = cc(np.asarray(L0_attn_bk, np.float32).reshape(1, D))
    bv = cc(np.asarray(L0_attn_bv, np.float32).reshape(1, D))
    bo = cc(np.asarray(L0_attn_bo, np.float32).reshape(1, D))
    wg = np.asarray(L0_ffn_w_gate, np.float32)
    wu = np.asarray(L0_ffn_w_up, np.float32)
    wd = np.asarray(L0_ffn_w_down, np.float32)
    wg_n = cc(wg)
    wu_n = cc(wu)
    wd_n = cc(wd)
    cT = cc(np.asarray(L0_c, np.float32).T)
    wmod = np.asarray(L0_wmod, np.float32)
    clipx = cc(np.asarray(L0_ffn_clip_x, np.float32).reshape(1, 1))
    clipd = cc(np.asarray(L0_ffn_clip_down, np.float32).reshape(1, 1))

    if _NC_CACHE[0] is None:
        _NC_CACHE[0] = build_program()
    nc = _NC_CACHE[0]

    in_maps = []
    for c in range(NC):
        sl = slice(c * TB, (c + 1) * TB)
        in_maps.append({
            "x": cc(xfm[:, :, sl]).reshape(NB * D, TB),
            "wqT": wqT, "wkT": wkT, "wvT": wvT, "woT": woT,
            "bq": bq, "bk": bk, "bv": bv, "bo": bo,
            "wg": wg_n, "wu": wu_n, "wdn": wd_n,
            "cT": cc(cT[c * 128:(c + 1) * 128, :]),
            "wmods": cc(wmod[c * 128:(c + 1) * 128, :]),
            "clipx": clipx, "clipd": clipd,
        })
    res = run_bass_kernel_spmd(nc, in_maps, core_ids=list(range(NC)))
    parts = [np.asarray(res.results[c]["out"]).reshape(NB, D, TB) for c in range(NC)]
    full = np.concatenate(parts, axis=2)
    return np.ascontiguousarray(full.transpose(0, 2, 1))


# version 27
# speedup vs baseline: 1.0104x; 1.0104x over previous
import numpy as np
import concourse.bass as bass
import concourse.mybir as mybir
from concourse.bass_utils import run_bass_kernel_spmd

F32 = mybir.dt.float32
BF16 = mybir.dt.bfloat16
I32 = mybir.dt.int32
AF = mybir.ActivationFunctionType
ALU = mybir.AluOpType
AX = mybir.AxisListType

NC = 8
D = 1024
DC = 8
II = 2816
IC = 22
TB = 1024
NB = 2
MAGIC_K = float(1.5 * (2.0**82 - 2.0**53))
MAGIC_LO = float(1.5 * 2.0**23)

ENGINES = ("sync", "vector", "scalar", "tensor", "gpsimd")
V, S, T, G, Y = "vector", "scalar", "tensor", "gpsimd", "sync"


class Buf:
    __slots__ = ("wr", "rd")

    def __init__(self):
        self.wr = None
        self.rd = []

    def rwaits(self):
        return [self.wr] if self.wr else []

    def wwaits(self):
        return ([self.wr] if self.wr else []) + self.rd

    def did_r(self, tok):
        self.rd.append(tok)

    def did_w(self, tok):
        self.wr = tok
        self.rd = []


DMA_ROT = {"sync": 12, "gpsimd": 8}


class Prog:
    def __init__(self, nc):
        self.nc = nc
        self.qs = {e: [] for e in ENGINES}
        self.cnt = {}
        self.handles = {}
        self.watermark = {e: {} for e in ENGINES}
        self.sem_names = []
        self.dmarr = {e: 0 for e in ENGINES}

    def sem(self, name):
        self.cnt[name] = 0
        self.sem_names.append(name)

    def _waits(self, eng, waits):
        merged = {}
        for w in waits:
            if w is None:
                continue
            s, v = w
            if v <= 0:
                continue
            if merged.get(s, -1) < v:
                merged[s] = v
        for s, v in sorted(merged.items()):
            if self.watermark[eng].get(s, -1) >= v:
                continue
            self.watermark[eng][s] = v
            self.qs[eng].append(lambda e, s=s, v=v: e.wait_ge(self.handles[s], v))

    def op(self, eng, build, reads=(), writes=(), extra=(), sem=None, n=1):
        waits = list(extra)
        for b in reads:
            waits += b.rwaits()
        for b in writes:
            waits += b.wwaits()
        self._waits(eng, waits)
        s = sem if sem is not None else eng
        self.cnt[s] += n
        tok = (s, self.cnt[s])
        self.qs[eng].append(lambda e, b=build, s=s, n=n: b(e).then_inc(self.handles[s], n))
        for b in reads:
            b.did_r(tok)
        for b in writes:
            b.did_w(tok)
        return tok

    def dma(self, eng, out, in_, reads=(), writes=(), extra=(), sem=None, slow=False):
        if slow:
            bld = lambda e, o=out, i=in_: e.dma_start(out=o, in_=i,
                                                      allow_slow_non_contiguous=True)
        else:
            bld = lambda e, o=out, i=in_: e.dma_start(out=o, in_=i)
        if sem is None:
            r = DMA_ROT[eng]
            s = "%sq%d" % (eng[:2], self.dmarr[eng] % r)
            self.dmarr[eng] += 1
            self._waits(eng, [(s, self.cnt[s])])
            sem = s
        return self.op(eng, bld, reads=reads, writes=writes, extra=extra, sem=sem, n=16)

    def mark(self, eng):
        return (eng, self.cnt[eng])

    def barrier(self):
        toks = [(s, self.cnt[s]) for s in self.sem_names]
        for e in ENGINES:
            self._waits(e, toks)


def build_program():
    nc = bass.Bass("TRN2", target_bir_lowering=False, debug=False)
    ap = lambda t: t.ap()
    x_d = ap(nc.dram_tensor("x", [NB * D, TB], F32, kind="ExternalInput"))
    wqT_d = ap(nc.dram_tensor("wqT", [D, D], F32, kind="ExternalInput"))
    wkT_d = ap(nc.dram_tensor("wkT", [D, D], F32, kind="ExternalInput"))
    wvT_d = ap(nc.dram_tensor("wvT", [D, D], F32, kind="ExternalInput"))
    woT_d = ap(nc.dram_tensor("woT", [D, D], F32, kind="ExternalInput"))
    bq_d = ap(nc.dram_tensor("bq", [1, D], F32, kind="ExternalInput"))
    bk_d = ap(nc.dram_tensor("bk", [1, D], F32, kind="ExternalInput"))
    bv_d = ap(nc.dram_tensor("bv", [1, D], F32, kind="ExternalInput"))
    bo_d = ap(nc.dram_tensor("bo", [1, D], F32, kind="ExternalInput"))
    wg_d = ap(nc.dram_tensor("wg", [II, D], F32, kind="ExternalInput"))
    wu_d = ap(nc.dram_tensor("wu", [II, D], F32, kind="ExternalInput"))
    wdn_d = ap(nc.dram_tensor("wdn", [D, II], F32, kind="ExternalInput"))
    cT_d = ap(nc.dram_tensor("cT", [128, NB], F32, kind="ExternalInput"))
    wmods_d = ap(nc.dram_tensor("wmods", [128, 6 * D], F32, kind="ExternalInput"))
    clipx_d = ap(nc.dram_tensor("clipx", [1, 1], F32, kind="ExternalInput"))
    clipd_d = ap(nc.dram_tensor("clipd", [1, 1], F32, kind="ExternalInput"))
    out_d = ap(nc.dram_tensor("out", [NB * D, TB], F32, kind="ExternalOutput"))

    ar0i = ap(nc.dram_tensor("ar0i", [128, 96], F32))
    ar0o = ap(nc.dram_tensor("ar0o", [128, 96], F32))
    kvari = ap(nc.dram_tensor("kvari", [NB * DC * 128, 1025], BF16))
    kvaro = ap(nc.dram_tensor("kvaro", [NB * DC * 128, 1025], BF16))
    ar2i = ap(nc.dram_tensor("ar2i", [1, 1], F32))
    ar2o = ap(nc.dram_tensor("ar2o", [1, 1], F32))
    ar3i = ap(nc.dram_tensor("ar3i", [1, 1], F32))
    ar3o = ap(nc.dram_tensor("ar3o", [1, 1], F32))
    rows_d = ap(nc.dram_tensor("rows_d", [16, TB], F32))
    scal_d = ap(nc.dram_tensor("scal_d", [8, 1], F32))
    x2sp = ap(nc.dram_tensor("x2sp", [NB * D, TB], F32))
    sisp = ap(nc.dram_tensor("sisp", [NB * IC * 128, TB], BF16))

    P = Prog(nc)
    for s in ("sync", "vector", "scalar", "tensor", "gpsimd", "coll"):
        P.sem(s)
    for e, r in DMA_ROT.items():
        for i in range(r):
            P.sem("%sq%d" % (e[:2], i))

    from contextlib import ExitStack
    with ExitStack() as stk:
        sbt = lambda name, shape, dt: stk.enter_context(nc.sbuf_tensor(name, shape, dt))
        xr_t = [sbt(f"xr{i}", [128, TB], F32) for i in range(3)]
        abA_t = sbt("abA", [128, TB], F32)
        abB_t = sbt("abB", [128, TB], F32)
        rz_t = sbt("rz", [128, TB], F32)
        big_t = sbt("big", [128, 5632], F32)
        t512_t = [sbt(f"t512_{i}", [128, 512], F32) for i in range(6)]
        t1k_t = [sbt(f"t1k_{i}", [128, TB], F32) for i in range(3)]
        kvt_t = [sbt(f"kvt{i}", [128, 1025], BF16) for i in range(2)]
        rows_t = sbt("rows", [1, 4 * TB + 2], F32)
        smallc_t = sbt("smallc", [128, 64], F32)
        ident_t = sbt("ident", [128, 128], F32)
        io1_t = sbt("io1", [128, 128], I32)
        io2_t = sbt("io2", [128, 128], I32)
        modp_t = sbt("modp", [128, 96], F32)
        modc_t = sbt("modc", [128, 96], F32)
        siluc_t = sbt("siluc", [128, 4], F32)
        mcol_t = sbt("mcol", [128, 128], F32)
        mxb_t = sbt("mxb", [128, 128], F32)
        rbh_t = [sbt(f"rbh{i}", [128, TB], BF16) for i in range(16)]
        rbk_t = [sbt(f"rbk{i}", [128, TB], BF16) for i in range(16)]
        rbw_t = [sbt(f"rbw{i}", [128, TB], BF16) for i in range(16)]
        rbx_t = sbt("rbx", [128, 4800], BF16)
        bsc_t = [sbt(f"bsc{i}", [128, TB], BF16) for i in range(2)]
        psA = [stk.enter_context(nc.psum_tensor(f"pA{k}", [128, 1024], F32))
               for k in range(3)]
        psT = [stk.enter_context(nc.psum_tensor(f"pT{k}", [128, 512], F32))
               for k in range(2)]
        psb = [psA[0][:, 0:512], psA[0][:, 512:1024],
               psA[1][:, 0:512], psA[1][:, 512:1024],
               psA[2][:, 0:512], psA[2][:, 512:1024],
               psT[0][:, 0:512], psT[1][:, 0:512]]
        for s in P.sem_names:
            P.handles[s] = stk.enter_context(nc.semaphore(f"sem_{s}"))
        block = stk.enter_context(nc.Block())

        B_xr = [Buf() for _ in range(3)]
        B_abA, B_abB, B_rz = Buf(), Buf(), Buf()
        big_guard = [[]]

        def big_bufs(k):
            toks = list(big_guard[0])
            bufs = [Buf() for _ in range(k)]
            return bufs, toks

        def big_close(bufs):
            toks = []
            for b in bufs:
                toks += b.wwaits()
            big_guard[0] = toks

        B_t512 = [Buf() for _ in range(6)]
        B_t1k = [Buf() for _ in range(3)]
        B_kvt = [Buf() for _ in range(2)]
        B_rows = {}

        def rowbuf(i):
            if i not in B_rows:
                B_rows[i] = Buf()
            return B_rows[i]

        B_col = {}
        COLIDX = {}

        def col(name):
            if name not in COLIDX:
                COLIDX[name] = len(COLIDX)
                B_col[name] = Buf()
            i = COLIDX[name]
            return smallc_t[:, i:i + 1], B_col[name]

        B_ident, B_io1, B_io2 = Buf(), Buf(), Buf()
        B_modp, B_modc, B_siluc, B_mcol, B_mxb = Buf(), Buf(), Buf(), Buf(), Buf()
        B_rbh = [Buf() for _ in range(16)]
        B_rbk = [Buf() for _ in range(16)]
        B_rbw = [Buf() for _ in range(16)]
        B_bq, B_bk, B_bv, B_bo = Buf(), Buf(), Buf(), Buf()
        B_o512, B_o128, B_oc = Buf(), Buf(), Buf()
        B_ks = [Buf() for _ in range(DC)]
        B_bsc = [Buf() for _ in range(2)]
        B_ps = [Buf() for _ in range(8)]
        B_x2d = [Buf() for _ in range(NB * DC)]
        B_sid = [Buf() for _ in range(NB * IC)]
        B_rowd = [Buf() for _ in range(16)]
        B_scald = [Buf() for _ in range(8)]
        B_ar = {k: Buf() for k in ("0i", "0o", "1i", "1o", "2i", "2o", "3i", "3o")}

        bq_r = rbx_t[0:1, 0:1024]
        bk_r = rbx_t[0:1, 1024:2048]
        bv_r = rbx_t[0:1, 2048:3072]
        bo_r = rbx_t[0:1, 3072:4096]
        ones_r512 = rbx_t[0:1, 4096:4608]
        ones_r128 = rbx_t[0:1, 4608:4736]
        ones_c = rbx_t[:, 4740:4741]
        ks_cs = [rbx_t[:, 4750 + i:4751 + i] for i in range(DC)]

        magic, B_magic = col("magic")
        eps_c, B_eps = col("eps")
        sigx_c, B_sigx = col("sigx")
        sigd_c, B_sigd = col("sigd")
        rs2_c, B_rs2 = col("rs2")
        s2_c, B_s2 = col("s2")
        rs3_c, B_rs3 = col("rs3")
        s3_c, B_s3 = col("s3")
        kst_c, B_kst = col("kst")
        tsc_c, B_tsc = col("tsc")

        def row(i, n=TB):
            return rows_t[0:1, i * TB:i * TB + n]

        def mcolv(g, b, c):
            j = (g * 8 + c) * 2 + b
            return modc_t[:, j:j + 1]

        P.op(G, lambda e: e.iota(out=io1_t[:], pattern=[[0, 128]], channel_multiplier=1),
             writes=[B_io1])
        P.op(G, lambda e: e.iota(out=io2_t[:], pattern=[[1, 128]], channel_multiplier=0),
             writes=[B_io2])
        P.op(V, lambda e: e.tensor_tensor(out=ident_t[:], in0=io1_t[:], in1=io2_t[:],
                                          op=ALU.is_equal),
             reads=[B_io1, B_io2], writes=[B_ident])
        P.op(V, lambda e: e.memset(mxb_t[:], 0.0), writes=[B_mxb])
        P.op(V, lambda e: e.memset(mcol_t[:], 0.0), writes=[B_mcol])
        P.op(V, lambda e: e.memset(magic, 1.0), writes=[B_magic])
        P.op(V, lambda e: e.tensor_scalar(out=eps_c, in0=magic, scalar1=float(2.0**-30),
                                          scalar2=1.0, op0=ALU.add, op1=ALU.subtract),
             reads=[B_magic], writes=[B_eps])
        P.op(V, lambda e: e.tensor_scalar(out=magic, in0=eps_c, scalar1=MAGIC_K,
                                          scalar2=MAGIC_LO, op0=ALU.mult, op1=ALU.add),
             reads=[B_eps], writes=[B_magic])
        P.op(V, lambda e: e.memset(ones_r512, 1.0), writes=[B_o512])
        P.op(V, lambda e: e.memset(ones_r128, 1.0), writes=[B_o128])
        P.op(V, lambda e: e.memset(ones_c, 1.0), writes=[B_oc])
        P.dma(G, bq_r, bq_d, writes=[B_bq])
        P.dma(G, bk_r, bk_d, writes=[B_bk])
        P.dma(G, bv_r, bv_d, writes=[B_bv])
        P.dma(G, bo_r, bo_d, writes=[B_bo])
        P.dma(Y, t512_t[0][0:1, 0:1], clipx_d, writes=[B_t512[0]])
        P.dma(Y, t512_t[0][0:1, 1:2], clipd_d, writes=[B_t512[0]])
        P.op(S, lambda e: e.activation(out=t512_t[1][0:1, 0:1], in_=t512_t[0][0:1, 0:1],
                                       func=AF.Sigmoid), reads=[B_t512[0]], writes=[B_t512[1]])
        P.op(S, lambda e: e.activation(out=t512_t[1][0:1, 1:2], in_=t512_t[0][0:1, 1:2],
                                       func=AF.Sigmoid), reads=[B_t512[0]], writes=[B_t512[1]])
        P.dma(Y, scal_d[0:1, 0:1], t512_t[1][0:1, 0:1], reads=[B_t512[1]], writes=[B_scald[0]])
        P.dma(Y, scal_d[1:2, 0:1], t512_t[1][0:1, 1:2], reads=[B_t512[1]], writes=[B_scald[1]])
        P.dma(Y, sigx_c, scal_d[0:1, 0:1].partition_broadcast(128),
              reads=[B_scald[0]], writes=[B_sigx])
        P.dma(Y, sigd_c, scal_d[1:2, 0:1].partition_broadcast(128),
              reads=[B_scald[1]], writes=[B_sigd])

        B_bqc, B_boc = Buf(), Buf()
        bqc_cols = mcol_t[:, 120:128]
        boc_cols = mxb_t[:, 80:88]
        P.dma(G, bqc_cols, bq_d.rearrange("one (a p) -> p (a one)", p=128),
              writes=[B_bqc], slow=True)
        P.dma(G, boc_cols, bo_d.rearrange("one (a p) -> p (a one)", p=128),
              writes=[B_boc], slow=True)

        P.dma(Y, siluc_t[:, 0:2], cT_d, writes=[B_siluc])
        P.op(S, lambda e: e.activation(out=siluc_t[:, 2:4], in_=siluc_t[:, 0:2], func=AF.Silu),
             reads=[B_siluc], writes=[B_siluc])
        bw, bg0 = big_bufs(1)
        B_wmod = bw[0]
        for half in range(2):
            P.dma(Y, big_t[:, 0:3072], wmods_d[:, half * 3072:(half + 1) * 3072],
                  writes=[B_wmod], extra=bg0)
            for j in range(24):
                jg = half * 24 + j
                pb = psb[6 + (jg % 2)]
                pB = B_ps[6 + (jg % 2)]
                P.op(T, lambda e, j=j, pb=pb: e.matmul(
                    out=pb[:, 0:2], lhsT=big_t[:, j * 128:(j + 1) * 128], rhs=siluc_t[:, 2:4],
                    start=True, stop=True),
                    reads=[B_wmod, B_siluc], writes=[pB])
                P.op(V, lambda e, jg=jg, pb=pb: e.tensor_copy(
                    out=modp_t[:, 2 * jg:2 * jg + 2], in_=pb[:, 0:2]),
                    reads=[pB], writes=[B_modp])
        big_close([B_wmod])
        P.dma(G, ar0i, modp_t[:], reads=[B_modp], writes=[B_ar["0i"]])
        P.op(G, lambda e: e.collective_compute(
            "AllReduce", ALU.add, replica_groups=[list(range(NC))], ins=[ar0i], outs=[ar0o]),
            reads=[B_ar["0i"]], writes=[B_ar["0o"]], sem="coll")
        P.dma(G, modc_t[:], ar0o, reads=[B_ar["0o"]], writes=[B_modc])

        def clampg(g0):
            sl = modc_t[:, g0 * 16:g0 * 16 + 16]
            a, b_, c_, d_ = (t1k_t[0][:, k * 16:(k + 1) * 16] for k in range(4))
            P.op(V, lambda e: e.tensor_scalar(out=a, in0=sl, scalar1=-1.0, scalar2=None,
                                              op0=ALU.mult), reads=[B_modc], writes=[B_t1k[0]])
            P.op(V, lambda e: e.tensor_tensor(out=b_, in0=sl, in1=a, op=ALU.max),
                 reads=[B_modc, B_t1k[0]], writes=[B_t1k[0]])
            P.op(V, lambda e: e.tensor_scalar(out=b_, in0=b_, scalar1=0.25, scalar2=None,
                                              op0=ALU.max), reads=[B_t1k[0]], writes=[B_t1k[0]])
            P.op(V, lambda e: e.tensor_scalar(out=c_, in0=sl, scalar1=0.0, scalar2=None,
                                              op0=ALU.is_ge), reads=[B_modc], writes=[B_t1k[0]])
            P.op(V, lambda e: e.tensor_scalar(out=d_, in0=c_, scalar1=2.0, scalar2=-1.0,
                                              op0=ALU.mult, op1=ALU.add),
                 reads=[B_t1k[0]], writes=[B_t1k[0]])
            P.op(V, lambda e: e.tensor_tensor(out=sl, in0=b_, in1=d_, op=ALU.mult),
                 reads=[B_t1k[0]], writes=[B_modc])
        clampg(2)
        clampg(5)
        P.op(V, lambda e: e.tensor_scalar(out=modc_t[:, 16:32], in0=modc_t[:, 16:32],
                                          scalar1=1.0, scalar2=None, op0=ALU.add),
             reads=[B_modc], writes=[B_modc])
        P.op(V, lambda e: e.tensor_scalar(out=modc_t[:, 64:80], in0=modc_t[:, 64:80],
                                          scalar1=1.0, scalar2=None, op0=ALU.add),
             reads=[B_modc], writes=[B_modc])

        for i in range(DC):
            P.dma(G, rbw_t[i][:], wkT_d[i * 128:(i + 1) * 128, :], writes=[B_rbw[i]])
        for i in range(DC):
            P.dma(G, rbw_t[8 + i][:], wvT_d[i * 128:(i + 1) * 128, :], writes=[B_rbw[8 + i]])

        xri = [0]

        def xload(src_d, idx, extra_src=None):
            k = xri[0] % 3
            xri[0] += 1
            ex = [] if extra_src is None else extra_src.rwaits()
            tok = P.dma(Y, xr_t[k][:], src_d[idx * 128:(idx + 1) * 128, :],
                        writes=[B_xr[k]], extra=ex)
            if extra_src is not None:
                extra_src.did_r(tok)
            return xr_t[k][:], B_xr[k]

        def ln_pass(b, src_d, sc0, sc1, Bsc0, Bsc1, src_bufs, drow):
            sum_sl = [psb[0][0:1, 0:512], psb[1][0:1, 0:512]]
            sq_sl = [psb[2][0:1, 0:512], psb[3][0:1, 0:512]]
            scp = [(sc0, sc1, Bsc0, Bsc1), (rbh_t[12][:], rbh_t[13][:], B_rbh[12], B_rbh[13])]
            for c in range(DC):
                s0, s1, Bs0, Bs1 = scp[c % 2]
                buf, Bb = xload(src_d, b * DC + c,
                                src_bufs[b * DC + c] if src_bufs else None)
                P.op(S, lambda e, buf=buf, s=s0: e.activation(out=s[:], in_=buf, func=AF.Copy),
                     reads=[Bb], writes=[Bs0])
                P.op(S, lambda e, buf=buf, s=s1: e.activation(out=s[:], in_=buf, func=AF.Square),
                     reads=[Bb], writes=[Bs1])
                for h in range(2):
                    P.op(T, lambda e, h=h, c=c, s=s0: e.matmul(
                        out=sum_sl[h], lhsT=ones_c, rhs=s[:, h * 512:(h + 1) * 512],
                        start=(c == 0), stop=(c == DC - 1)),
                        reads=[Bs0, B_oc], writes=[B_ps[h]] if c == DC - 1 or c == 0 else [])
                    P.op(T, lambda e, h=h, c=c, s=s1: e.matmul(
                        out=sq_sl[h], lhsT=ones_c, rhs=s[:, h * 512:(h + 1) * 512],
                        start=(c == 0), stop=(c == DC - 1)),
                        reads=[Bs1, B_oc], writes=[B_ps[2 + h]] if c == DC - 1 or c == 0 else [])
            r_M, r_T, r_A, r_B = (row(i) for i in range(4))
            Bm, Bt, BA, BB = (rowbuf(i) for i in range(4))
            P.op(V, lambda e: e.tensor_copy(out=r_M[:, 0:1024], in_=psA[0][0:1, 0:1024]),
                 reads=[B_ps[0], B_ps[1]], writes=[Bm])
            P.op(V, lambda e: e.tensor_copy(out=r_T[:, 0:1024], in_=psA[1][0:1, 0:1024]),
                 reads=[B_ps[2], B_ps[3]], writes=[Bt])
            P.op(V, lambda e: e.tensor_scalar(out=r_M, in0=r_M, scalar1=1.0 / D,
                                              scalar2=None, op0=ALU.mult),
                 reads=[Bm], writes=[Bm])
            P.op(V, lambda e: e.tensor_tensor(out=r_A, in0=r_M, in1=r_M, op=ALU.mult),
                 reads=[Bm], writes=[BA])
            P.op(V, lambda e: e.scalar_tensor_tensor(out=r_T, in0=r_T, scalar=1.0 / D,
                                                     in1=r_A, op0=ALU.mult, op1=ALU.subtract),
                 reads=[Bt, BA], writes=[Bt])
            P.op(V, lambda e: e.tensor_scalar(out=r_T, in0=r_T, scalar1=1e-5, scalar2=None,
                                              op0=ALU.add), reads=[Bt], writes=[Bt])
            BdA, BdB = rowbuf(100 + drow), rowbuf(101 + drow)
            P.dma(G, rows_d[drow:drow + 1, :], r_T, reads=[Bt], writes=[BdA])
            P.dma(G, rows_d[drow + 1:drow + 2, :], r_M, reads=[Bm], writes=[BdB])

        def ln_bcast(drow):
            BdA, BdB = rowbuf(100 + drow), rowbuf(101 + drow)
            P.dma(G, abA_t[:], rows_d[drow:drow + 1, :].partition_broadcast(128),
                  reads=[BdA], writes=[B_abA])
            P.dma(G, abB_t[:], rows_d[drow + 1:drow + 2, :].partition_broadcast(128),
                  reads=[BdB], writes=[B_abB])
            P.op(S, lambda e: e.activation(out=abA_t[:], in_=abA_t[:], func=AF.Sqrt),
                 reads=[B_abA], writes=[B_abA])
            P.op(V, lambda e: e.reciprocal(out=abA_t[:], in_=abA_t[:]),
                 reads=[B_abA], writes=[B_abA])
            P.op(V, lambda e: e.scalar_tensor_tensor(out=abB_t[:], in0=abB_t[:],
                                                     scalar=-1.0, in1=abA_t[:],
                                                     op0=ALU.mult, op1=ALU.mult),
                 reads=[B_abB, B_abA], writes=[B_abB])

        for b in range(NB):
            ln_pass(b, x_d, bsc_t[0], bsc_t[1], B_bsc[0], B_bsc[1], None, 4 + 2 * b)
        for b in range(NB):
            ln_bcast(4 + 2 * b)
            for c in range(DC):
                buf, Bb = xload(x_d, b * DC + c)
                P.op(S, lambda e, b=b, c=c: e.activation(
                    out=t1k_t[2][:], in_=abB_t[:], func=AF.Identity,
                    scale=mcolv(1, b, c), bias=mcolv(0, b, c)),
                    reads=[B_abB, B_modc], writes=[B_t1k[2]])
                P.op(G, lambda e, buf=buf: e.tensor_tensor(
                    out=t1k_t[1][:], in0=buf, in1=abA_t[:], op=ALU.mult),
                    reads=[Bb, B_abA], writes=[B_t1k[1]])
                P.op(V, lambda e, b=b, c=c: e.scalar_tensor_tensor(
                    out=rbh_t[b * DC + c][:], in0=t1k_t[1][:], scalar=mcolv(1, b, c),
                    in1=t1k_t[2][:], op0=ALU.mult, op1=ALU.add),
                    reads=[B_t1k[1], B_t1k[2], B_modc], writes=[B_rbh[b * DC + c]])

        kv_dma_toks = [[], []]
        B_kvd = [Buf() for _ in range(NB * DC)]
        B_ar1 = [Buf(), Buf()]
        for b in range(NB):
            for n in range(DC):
                p2 = n % 2
                kb = [psb[0 + 4 * p2], psb[1 + 4 * p2]]
                kB = [B_ps[0 + 4 * p2], B_ps[1 + 4 * p2]]
                vb = [psb[2 + 4 * p2], psb[3 + 4 * p2]]
                vB = [B_ps[2 + 4 * p2], B_ps[3 + 4 * p2]]
                kfull = psA[0][:] if p2 == 0 else psA[2][:]
                for i in range(DC):
                    lhs = rbh_t[b * DC + i][:, n * 128:(n + 1) * 128]
                    for h in range(2):
                        P.op(T, lambda e, lhs=lhs, h=h, i=i, kb=kb: e.matmul(
                            out=kb[h][:], lhsT=lhs, rhs=rbw_t[i][:, h * 512:(h + 1) * 512],
                            start=(i == 0), stop=False),
                            reads=[B_rbh[b * DC + i], B_rbw[i]],
                            writes=[kB[h]] if i == 0 else [])
                        P.op(T, lambda e, lhs=lhs, h=h, i=i, vb=vb: e.matmul(
                            out=vb[h][:], lhsT=lhs, rhs=rbw_t[8 + i][:, h * 512:(h + 1) * 512],
                            start=(i == 0), stop=False),
                            reads=[B_rbh[b * DC + i], B_rbw[8 + i]],
                            writes=[vB[h]] if i == 0 else [])
                for h in range(2):
                    P.op(T, lambda e, h=h, kb=kb: e.matmul(
                        out=kb[h][:], lhsT=ones_r128, rhs=bk_r[0:1, h * 512:(h + 1) * 512],
                        start=False, stop=True),
                        reads=[B_o128, B_bk], writes=[kB[h]])
                    P.op(T, lambda e, h=h, vb=vb: e.matmul(
                        out=vb[h][:], lhsT=ones_r128, rhs=bv_r[0:1, h * 512:(h + 1) * 512],
                        start=False, stop=True),
                        reads=[B_o128, B_bv], writes=[vB[h]])
                km_, Bkm = (t1k_t[0], B_t1k[0]) if p2 == 0 else (t1k_t[2], B_t1k[2])
                ke_, Bke = (t1k_t[1], B_t1k[1]) if p2 == 0 else (rz_t, B_rz)
                P.op(V, lambda e, kfull=kfull, km_=km_: e.tensor_scalar(
                    out=km_[:], in0=kfull, scalar1=0.0, scalar2=None, op0=ALU.min),
                    reads=kB, writes=[Bkm])
                P.op(S, lambda e, km_=km_, ke_=ke_: e.activation(
                    out=ke_[:], in_=km_[:], func=AF.Exp),
                    reads=[Bkm], writes=[Bke])
                P.op(V, lambda e, n=n, kfull=kfull, ke_=ke_: e.scalar_tensor_tensor(
                    out=rbk_t[n][:], in0=kfull, scalar=0.0,
                    in1=ke_[:], op0=ALU.max, op1=ALU.add),
                    reads=kB + [Bke], writes=[B_rbk[n]])
                if p2 == 0:
                    P.op(S, lambda e, n=n: e.activation(
                        out=rbk_t[8 + n][:], in_=psA[1][:],
                        func=AF.Copy), reads=vB, writes=[B_rbk[8 + n]])
                else:
                    for h in range(2):
                        P.op(S, lambda e, n=n, h=h, vb=vb: e.activation(
                            out=rbk_t[8 + n][:, h * 512:(h + 1) * 512], in_=vb[h][:],
                            func=AF.Copy), reads=[vB[h]], writes=[B_rbk[8 + n]])
            for dc in range(DC):
                p2 = dc % 2
                wb = [psb[4 - 4 * p2], psb[5 - 4 * p2]]
                wB = [B_ps[4 - 4 * p2], B_ps[5 - 4 * p2]]
                wfull = psA[2][:] if p2 == 0 else psA[0][:]
                ksb = psb[6 + p2]
                ksB = B_ps[6 + p2]
                for n in range(DC):
                    lhs = rbk_t[n][:, dc * 128:(dc + 1) * 128]
                    for h in range(2):
                        P.op(T, lambda e, lhs=lhs, n=n, h=h, wb=wb: e.matmul(
                            out=wb[h][:], lhsT=lhs, rhs=rbk_t[8 + n][:, h * 512:(h + 1) * 512],
                            start=(n == 0), stop=(n == DC - 1)),
                            reads=[B_rbk[n], B_rbk[8 + n]],
                            writes=[wB[h]] if n == 0 or n == DC - 1 else [])
                    P.op(T, lambda e, lhs=lhs, n=n, ksb=ksb: e.matmul(
                        out=ksb[:, 0:1], lhsT=lhs, rhs=ones_c,
                        start=(n == 0), stop=(n == DC - 1)),
                        reads=[B_rbk[n], B_oc],
                        writes=[ksB] if n == 0 or n == DC - 1 else [])
                kvb_ = kvt_t[dc % 2][:]
                Bk = B_kvt[dc % 2]
                P.op(V, lambda e, kvb_=kvb_, wfull=wfull: e.tensor_copy(
                    out=kvb_[0:128, 0:1024], in_=wfull),
                     reads=wB, writes=[Bk])
                P.op(V, lambda e, kvb_=kvb_, ksb=ksb: e.tensor_copy(
                    out=kvb_[0:128, 1024:1025], in_=ksb[:, 0:1]),
                     reads=[ksB], writes=[Bk])
                kv_dma_toks[b].append(P.dma(
                    Y, kvari[(b * DC + dc) * 128:(b * DC + dc + 1) * 128, :], kvb_,
                    reads=[Bk], writes=[B_kvd[b * DC + dc]]))
            P.op(G, lambda e, b=b: e.collective_compute(
                "AllReduce", ALU.add, replica_groups=[list(range(NC))],
                ins=[kvari[b * DC * 128:(b + 1) * DC * 128, :]],
                outs=[kvaro[b * DC * 128:(b + 1) * DC * 128, :]]),
                reads=B_kvd[b * DC:(b + 1) * DC], writes=[B_ar1[b]], sem="coll",
                extra=kv_dma_toks[b])

        for i in range(DC):
            P.dma(G, rbw_t[i][:], wqT_d[i * 128:(i + 1) * 128, :], writes=[B_rbw[i]])
        for b in range(NB):
            for oc in range(DC):
                pq = [psb[0 + 2 * (oc % 2)], psb[1 + 2 * (oc % 2)]]
                pB = [B_ps[0 + 2 * (oc % 2)], B_ps[1 + 2 * (oc % 2)]]
                for i in range(DC):
                    for h in range(2):
                        P.op(T, lambda e, i=i, oc=oc, h=h, b=b, pq=pq: e.matmul(
                            out=pq[h][:], lhsT=rbw_t[i][:, oc * 128:(oc + 1) * 128],
                            rhs=rbh_t[b * DC + i][:, h * 512:(h + 1) * 512],
                            start=(i == 0), stop=(i == DC - 1)),
                            reads=[B_rbw[i], B_rbh[b * DC + i]],
                            writes=[pB[h]] if i == 0 or i == DC - 1 else [])
                pqf = psA[oc % 2][:]
                ts_, Bts = (t1k_t[0], B_t1k[0]) if oc % 2 == 0 else (t1k_t[2], B_t1k[2])
                te_, Bte = (t1k_t[1], B_t1k[1]) if oc % 2 == 0 else (rz_t, B_rz)
                bqc = bqc_cols[:, oc:oc + 1]
                P.op(V, lambda e, pqf=pqf, ts_=ts_, bqc=bqc: e.tensor_scalar(
                    out=ts_[:], in0=pqf, scalar1=bqc, scalar2=None, op0=ALU.add),
                    reads=pB + [B_bqc], writes=[Bts])
                P.op(V, lambda e, ts_=ts_, te_=te_: e.tensor_scalar(
                    out=te_[:], in0=ts_[:], scalar1=0.0, scalar2=None, op0=ALU.min),
                    reads=[Bts], writes=[Bte])
                P.op(S, lambda e, te_=te_: e.activation(
                    out=te_[:], in_=te_[:], func=AF.Exp),
                    reads=[Bte], writes=[Bte])
                P.op(V, lambda e, b=b, oc=oc, ts_=ts_, te_=te_: e.scalar_tensor_tensor(
                    out=rbk_t[b * DC + oc][:], in0=ts_[:],
                    scalar=0.0, in1=te_[:], op0=ALU.max, op1=ALU.add),
                    reads=[Bts, Bte], writes=[B_rbk[b * DC + oc]])

        for i in range(DC):
            P.dma(G, rbw_t[8 + i][:], woT_d[i * 128:(i + 1) * 128, :], writes=[B_rbw[8 + i]])
        for b in range(NB):
            kvt_set = rbh_t[0:8] if b == 0 else rbw_t[0:8]
            B_kvt_set = B_rbh[0:8] if b == 0 else B_rbw[0:8]
            kvld = []
            for dc in range(DC):
                t = P.dma(G, kvt_set[dc][:],
                          kvaro[(b * DC + dc) * 128:(b * DC + dc + 1) * 128, 0:1024],
                          reads=[B_ar1[b]], writes=[B_kvt_set[dc]])
                kvld.append(t)
            ks8 = mcol_t[:, 112:120]
            P.dma(G, ks8,
                  kvaro[b * DC * 128:(b + 1) * DC * 128, 1024:1025]
                  .rearrange("(a p) one -> p (a one)", p=128),
                  reads=[B_ar1[b]], writes=[B_kst], slow=True)
            P.op(V, lambda e, ks8=ks8: e.tensor_scalar(
                out=ks8, in0=ks8, scalar1=1e-6, scalar2=None, op0=ALU.add),
                reads=[B_kst], writes=[B_kst])
            P.op(V, lambda e, ks8=ks8: e.tensor_copy(out=rbx_t[:, 4750:4758], in_=ks8),
                 reads=[B_kst], writes=B_ks)
            zsl = [psb[4][0:1, 0:512], psb[5][0:1, 0:512]]
            for dc in range(DC):
                for h in range(2):
                    P.op(T, lambda e, dc=dc, h=h, b=b, zsl=zsl: e.matmul(
                        out=zsl[h], lhsT=ks_cs[dc], rhs=rbk_t[b * DC + dc][:, h * 512:(h + 1) * 512],
                        start=(dc == 0), stop=(dc == DC - 1)),
                        reads=[B_ks[dc], B_rbk[b * DC + dc]],
                        writes=[B_ps[4 + h]] if dc == 0 or dc == DC - 1 else [])
            Bz8, Bz9 = rowbuf(0), rowbuf(1)
            P.op(V, lambda e: e.tensor_copy(out=row(0)[:, 0:1024], in_=psA[2][0:1, 0:1024]),
                 reads=[B_ps[4], B_ps[5]], writes=[Bz8])
            P.op(V, lambda e: e.reciprocal(out=row(1), in_=row(0)), reads=[Bz8], writes=[Bz9])
            Bz9d = rowbuf(109)
            P.dma(Y, rows_d[9:10, :], row(1), reads=[Bz9], writes=[Bz9d])
            P.dma(Y, rz_t[:], rows_d[9:10, :].partition_broadcast(128),
                  reads=[Bz9d], writes=[B_rz])
            for ec in range(DC):
                pc = [psb[0 + 2 * (ec % 2)], psb[1 + 2 * (ec % 2)]]
                pB = [B_ps[0 + 2 * (ec % 2)], B_ps[1 + 2 * (ec % 2)]]
                for dc in range(DC):
                    for h in range(2):
                        P.op(T, lambda e, dc=dc, ec=ec, h=h, b=b, pc=pc,
                             kv=kvt_set: e.matmul(
                            out=pc[h][:], lhsT=kv[dc][:, ec * 128:(ec + 1) * 128],
                            rhs=rbk_t[b * DC + dc][:, h * 512:(h + 1) * 512],
                            start=(dc == 0), stop=(dc == DC - 1)),
                            reads=[B_kvt_set[dc], B_rbk[b * DC + dc]],
                            writes=[pB[h]] if dc == 0 or dc == DC - 1 else [])
                P.op(V, lambda e, ec=ec: e.tensor_tensor(
                    out=rbh_t[8 + ec][:], in0=psA[ec % 2][:],
                    in1=rz_t[:], op=ALU.mult),
                    reads=pB + [B_rz], writes=[B_rbh[8 + ec]])
            for oc in range(DC):
                po = [psb[0 + 2 * (oc % 2)], psb[1 + 2 * (oc % 2)]]
                pB = [B_ps[0 + 2 * (oc % 2)], B_ps[1 + 2 * (oc % 2)]]
                for ec in range(DC):
                    for h in range(2):
                        P.op(T, lambda e, ec=ec, oc=oc, h=h, po=po: e.matmul(
                            out=po[h][:], lhsT=rbw_t[8 + ec][:, oc * 128:(oc + 1) * 128],
                            rhs=rbh_t[8 + ec][:, h * 512:(h + 1) * 512],
                            start=(ec == 0), stop=(ec == DC - 1)),
                            reads=[B_rbw[8 + ec], B_rbh[8 + ec]],
                            writes=[pB[h]] if ec == 0 or ec == DC - 1 else [])
                buf, Bb = xload(x_d, b * DC + oc)
                x2t = t1k_t[2][:]
                P.op(V, lambda e, b=b, oc=oc, x2t=x2t: e.tensor_scalar(
                    out=x2t, in0=psA[oc % 2][:], scalar1=boc_cols[:, oc:oc + 1],
                    scalar2=mcolv(2, b, oc), op0=ALU.add, op1=ALU.mult),
                    reads=pB + [B_boc, B_modc], writes=[B_t1k[2]])
                P.op(V, lambda e, x2t=x2t, buf=buf: e.tensor_tensor(
                    out=x2t, in0=x2t, in1=buf[0:128, 0:1024], op=ALU.add),
                    reads=[B_t1k[2], Bb], writes=[B_t1k[2]])
                P.dma(Y, x2sp[(b * DC + oc) * 128:(b * DC + oc + 1) * 128, :], x2t,
                      reads=[B_t1k[2]], writes=[B_x2d[b * DC + oc]])

        nmx = 0
        for b in range(NB):
            ln_pass(b, x2sp, bsc_t[0], bsc_t[1], B_bsc[0], B_bsc[1], B_x2d, 4 + 2 * b)
        for b in range(NB):
            ln_bcast(4 + 2 * b)
            for c in range(DC):
                buf, Bb = xload(x2sp, b * DC + c, B_x2d[b * DC + c])
                P.op(S, lambda e, b=b, c=c: e.activation(
                    out=t1k_t[2][:], in_=abB_t[:], func=AF.Identity,
                    scale=mcolv(4, b, c), bias=mcolv(3, b, c)),
                    reads=[B_abB, B_modc], writes=[B_t1k[2]])
                P.op(G, lambda e, buf=buf: e.tensor_tensor(
                    out=t1k_t[1][:], in0=buf, in1=abA_t[:], op=ALU.mult),
                    reads=[Bb, B_abA], writes=[B_t1k[1]])
                P.op(V, lambda e, b=b, c=c: e.scalar_tensor_tensor(
                    out=rbk_t[b * DC + c][:], in0=t1k_t[1][:], scalar=mcolv(4, b, c),
                    in1=t1k_t[2][:], op0=ALU.mult, op1=ALU.add),
                    reads=[B_t1k[1], B_t1k[2], B_modc], writes=[B_rbk[b * DC + c]])
                P.op(V, lambda e, b=b, c=c, nmx=nmx: e.tensor_reduce(
                    out=mxb_t[:, nmx:nmx + 1], in_=rbk_t[b * DC + c][:], axis=AX.X,
                    op=ALU.max, apply_absolute_value=True),
                    reads=[B_rbk[b * DC + c]], writes=[B_mxb])
                nmx += 1
        bufs6, bg6 = big_bufs(4)
        Bfg = bufs6[0:2]
        Bfu = bufs6[2:4]
        fg = [big_t[:, 0:1024], big_t[:, 1024:2048]]
        fu = [big_t[:, 2048:3072], big_t[:, 3072:4096]]
        Bsg = [B_rbk[0], B_rbk[1]]
        Bsu = [B_rbk[2], B_rbk[3]]
        Bsi = [B_rbk[4], B_rbk[5], B_rbk[6], B_rbk[7]]
        B_mg = [Buf(), Buf()]
        B_mu = [Buf(), Buf()]
        B_rg6 = [Buf(), Buf()]
        B_ru6 = [Buf(), Buf()]
        B_sg2 = [Buf(), Buf()]
        B_su2 = [Buf(), Buf()]
        sidx = [0]
        first6 = [bg6]

        def load6(ic):
            p6 = ic % 2
            P.dma(Y, fg[p6], wg_d[ic * 128:(ic + 1) * 128, :], writes=[Bfg[p6]],
                  extra=first6[0])
            P.dma(G, fu[p6], wu_d[ic * 128:(ic + 1) * 128, :], writes=[Bfu[p6]],
                  extra=first6[0])
            first6[0] = []

        def cols6(p6):
            return (mcol_t[:, 0 + p6:1 + p6], mcol_t[:, 2 + p6:3 + p6],
                    mcol_t[:, 4 + p6:5 + p6], mcol_t[:, 6 + p6:7 + p6],
                    mcol_t[:, 8 + p6:9 + p6], mcol_t[:, 10 + p6:11 + p6])

        QG = [bsc_t[0], rbk_t[4]]
        QU = [bsc_t[1], rbk_t[5]]
        BQG = [B_bsc[0], B_rbk[4]]
        BQU = [B_bsc[1], B_rbk[5]]

        def prep6q(ic):
            p6 = ic % 2
            fgb, fub = fg[p6], fu[p6]
            Bg, Bu = Bfg[p6], Bfu[p6]
            mgc, muc, rgc, ruc, sgc, suc = cols6(p6)
            for (fb, Bf, mc, Bm, rc, Br) in (
                    (fgb, Bg, mgc, B_mg[p6], rgc, B_rg6[p6]),
                    (fub, Bu, muc, B_mu[p6], ruc, B_ru6[p6])):
                P.op(V, lambda e, fb=fb, mc=mc: e.tensor_reduce(
                    out=mc, in_=fb, axis=AX.X, op=ALU.max, apply_absolute_value=True),
                    reads=[Bf], writes=[Bm])
                P.op(V, lambda e, mc=mc, rc=rc: e.reciprocal(out=rc, in_=mc),
                     reads=[Bm], writes=[Br])
                P.op(V, lambda e, fb=fb, rc=rc: e.tensor_scalar(
                    out=fb, in0=fb, scalar1=rc, scalar2=127.0, op0=ALU.mult, op1=ALU.mult),
                    reads=[Bf, Br], writes=[Bf])
                P.op(V, lambda e, fb=fb: e.tensor_scalar(
                    out=fb, in0=fb, scalar1=magic, scalar2=magic, op0=ALU.add,
                    op1=ALU.subtract), reads=[Bf, B_magic], writes=[Bf])

        def prep6t(ic):
            p6 = ic % 2
            fgb, fub = fg[p6], fu[p6]
            Bg, Bu = Bfg[p6], Bfu[p6]
            qg, qu = QG[p6], QU[p6]
            Bqg, Bqu = BQG[p6], BQU[p6]
            for hc in range(DC):
                for (fb, Bf, qt, Bq, pk) in ((fgb, Bg, qg, Bqg, 0), (fub, Bu, qu, Bqu, 1)):
                    pt = psT[pk]
                    pBt = B_ps[6 + pk]
                    P.op(T, lambda e, fb=fb, hc=hc, pt=pt: e.transpose(
                        out=pt[:, (hc % 4) * 128:(hc % 4 + 1) * 128],
                        in_=fb[:, hc * 128:(hc + 1) * 128],
                        identity=ident_t[:]), reads=[Bf, B_ident], writes=[pBt])
                    if hc % 4 == 3:
                        hg = hc // 4
                        P.op(S, lambda e, hg=hg, qt=qt, pt=pt: e.activation(
                            out=qt[:, hg * 512:(hg + 1) * 512], in_=pt[:, 0:512],
                            func=AF.Copy), reads=[pBt], writes=[Bq])

        def comp6(ic):
            p6 = ic % 2
            mgc, muc, rgc, ruc, sgc, suc = cols6(p6)
            qg, qu = QG[p6], QU[p6]
            Bqg, Bqu = BQG[p6], BQU[p6]
            P.op(V, lambda e, mgc=mgc, sgc=sgc: e.tensor_scalar(
                out=sgc, in0=mgc, scalar1=s2_c, scalar2=1.0 / 127.0, op0=ALU.mult,
                op1=ALU.mult), reads=[B_mg[p6], B_s2], writes=[B_sg2[p6]])
            P.op(V, lambda e, muc=muc, suc=suc: e.tensor_scalar(
                out=suc, in0=muc, scalar1=s2_c, scalar2=1.0 / 127.0, op0=ALU.mult,
                op1=ALU.mult), reads=[B_mu[p6], B_s2], writes=[B_su2[p6]])
            for b in range(NB):
                gA, uA = (psA[0], psA[1]) if b == 0 else (psA[2], psA[0])
                gb0, ub0 = (0, 2) if b == 0 else (4, 0)
                pBg = [B_ps[gb0], B_ps[gb0 + 1]]
                pBu = [B_ps[ub0], B_ps[ub0 + 1]]
                for hc in range(DC):
                    for h in range(2):
                        P.op(T, lambda e, hc=hc, h=h, b=b, qg=qg, gA=gA: e.matmul(
                            out=gA[:, h * 512:(h + 1) * 512],
                            lhsT=qg[:, hc * 128:(hc + 1) * 128],
                            rhs=rbh_t[b * DC + hc][:, h * 512:(h + 1) * 512],
                            start=(hc == 0), stop=(hc == DC - 1)),
                            reads=[Bqg, B_rbh[b * DC + hc]],
                            writes=[pBg[h]] if hc == 0 or hc == DC - 1 else [])
                        P.op(T, lambda e, hc=hc, h=h, b=b, qu=qu, uA=uA: e.matmul(
                            out=uA[:, h * 512:(h + 1) * 512],
                            lhsT=qu[:, hc * 128:(hc + 1) * 128],
                            rhs=rbh_t[b * DC + hc][:, h * 512:(h + 1) * 512],
                            start=(hc == 0), stop=(hc == DC - 1)),
                            reads=[Bqu, B_rbh[b * DC + hc]],
                            writes=[pBu[h]] if hc == 0 or hc == DC - 1 else [])
                t1x = t1k_t[sidx[0] % 2]
                Bt1x = B_t1k[sidx[0] % 2]
                P.op(S, lambda e, sgc=sgc, t1x=t1x, gA=gA: e.activation(
                    out=t1x[:], in_=gA[:], func=AF.Silu, scale=sgc),
                    reads=pBg + [B_sg2[p6]], writes=[Bt1x])
                P.op(V, lambda e, suc=suc, t1x=t1x, uA=uA: e.scalar_tensor_tensor(
                    out=t1x[:], in0=uA[:], scalar=suc, in1=t1x[:],
                    op0=ALU.mult, op1=ALU.mult),
                    reads=pBu + [B_su2[p6], Bt1x], writes=[Bt1x])
                P.op(V, lambda e, k=24 + sidx[0], t1x=t1x: e.tensor_reduce(
                    out=mxb_t[:, k:k + 1], in_=t1x[:], axis=AX.X, op=ALU.max,
                    apply_absolute_value=True), reads=[Bt1x], writes=[B_mxb])
                P.dma(G, sisp[(b * IC + ic) * 128:(b * IC + ic + 1) * 128, :], t1x[:],
                      reads=[Bt1x], writes=[B_sid[b * IC + ic]])
                sidx[0] += 1

        load6(0)
        load6(1)
        prep6q(0)
        prep6t(0)
        P.op(V, lambda e, n=nmx: e.tensor_reduce(
            out=mxb_t[:, 20:21], in_=mxb_t[:, 0:n], axis=AX.X, op=ALU.max),
            reads=[B_mxb], writes=[B_mxb])
        P.op(T, lambda e: e.transpose(out=psb[5][0:1, 0:128], in_=mxb_t[:, 20:21],
                                      identity=ident_t[:]),
             reads=[B_mxb, B_ident], writes=[B_ps[5]])
        Br10 = rowbuf(10)
        r10 = rows_t[0:1, 4 * TB:4 * TB + 1]
        P.op(V, lambda e: e.tensor_reduce(out=r10, in_=psb[5][0:1, 0:128],
                                          axis=AX.X, op=ALU.max),
             reads=[B_ps[5]], writes=[Br10])
        P.dma(Y, ar2i, r10, reads=[Br10], writes=[B_ar["2i"]])
        P.op(G, lambda e: e.collective_compute(
            "AllReduce", ALU.max, replica_groups=[list(range(NC))], ins=[ar2i], outs=[ar2o]),
            reads=[B_ar["2i"]], writes=[B_ar["2o"]], sem="coll")
        P.dma(G, tsc_c, ar2o.partition_broadcast(128), reads=[B_ar["2o"]], writes=[B_tsc])
        P.op(V, lambda e: e.tensor_tensor(out=tsc_c, in0=tsc_c, in1=sigx_c, op=ALU.mult),
             reads=[B_tsc, B_sigx], writes=[B_tsc])
        P.op(V, lambda e: e.tensor_scalar(out=s2_c, in0=tsc_c, scalar1=1.0 / 127.0,
                                          scalar2=None, op0=ALU.mult),
             reads=[B_tsc], writes=[B_s2])
        P.op(V, lambda e: e.reciprocal(out=rs2_c, in_=tsc_c), reads=[B_tsc], writes=[B_rs2])
        P.op(V, lambda e: e.tensor_scalar(out=rs2_c, in0=rs2_c, scalar1=127.0, scalar2=None,
                                          op0=ALU.mult), reads=[B_rs2], writes=[B_rs2])
        for b in range(NB):
            for c in range(DC):
                P.op(S, lambda e, b=b, c=c: e.activation(
                    out=t1k_t[2][:], in_=rbk_t[b * DC + c][:], func=AF.Identity,
                    bias=magic, scale=rs2_c),
                    reads=[B_rbk[b * DC + c], B_rs2, B_magic], writes=[B_t1k[2]])
                P.op(V, lambda e: e.tensor_scalar(
                    out=t1k_t[2][:], in0=t1k_t[2][:], scalar1=magic, scalar2=127.0,
                    op0=ALU.subtract, op1=ALU.min),
                    reads=[B_t1k[2], B_magic], writes=[B_t1k[2]])
                P.op(V, lambda e, b=b, c=c: e.tensor_scalar(
                    out=rbh_t[b * DC + c][:], in0=t1k_t[2][:], scalar1=-128.0, scalar2=None,
                    op0=ALU.max), reads=[B_t1k[2]], writes=[B_rbh[b * DC + c]])

        wdq_t = [rbw_t[i] for i in range(16)] + [rbk_t[8 + i] for i in range(6)]
        B_wdq = [B_rbw[i] for i in range(16)] + [B_rbk[8 + i] for i in range(6)]
        mdcols = mcol_t[:, 96:104]
        B_md = Buf()
        rd7_c, B_rd7 = col("rd7")
        md_sc = mcol_t[:, 104:107]
        B_mdsc = Buf()
        wsub = [(t1k_t[0][:, 0:1024], B_t1k[0], 1024),
                (t1k_t[1][:, 0:1024], B_t1k[1], 1024),
                (t1k_t[2][:, 0:768], B_t1k[2], 768)]
        for j in range(DC):
            for si3, (tt, Bt3, w) in enumerate(wsub):
                P.dma(Y if si3 % 2 == 0 else G, tt,
                      wdn_d[j * 128:(j + 1) * 128, si3 * 1024:si3 * 1024 + w],
                      writes=[Bt3])
                P.op(V, lambda e, tt=tt, si3=si3: e.tensor_reduce(
                    out=md_sc[:, si3:si3 + 1], in_=tt, axis=AX.X, op=ALU.max,
                    apply_absolute_value=True), reads=[Bt3], writes=[B_mdsc])
            mdc = mdcols[:, j:j + 1]
            P.op(V, lambda e, mdc=mdc: e.tensor_tensor(
                out=mdc, in0=md_sc[:, 0:1], in1=md_sc[:, 1:2], op=ALU.max),
                reads=[B_mdsc], writes=[B_md])
            P.op(V, lambda e, mdc=mdc: e.tensor_tensor(
                out=mdc, in0=mdc, in1=md_sc[:, 2:3], op=ALU.max),
                reads=[B_md, B_mdsc], writes=[B_md])
            P.op(V, lambda e, mdc=mdc: e.reciprocal(out=rd7_c, in_=mdc),
                 reads=[B_md], writes=[B_rd7])
            for (tt, Bt3, w) in wsub:
                P.op(V, lambda e, tt=tt: e.tensor_scalar(out=tt, in0=tt, scalar1=rd7_c,
                                                         scalar2=127.0, op0=ALU.mult,
                                                         op1=ALU.mult),
                     reads=[Bt3, B_rd7], writes=[Bt3])
                P.op(V, lambda e, tt=tt: e.tensor_scalar(out=tt, in0=tt, scalar1=magic,
                                                         scalar2=magic, op0=ALU.add,
                                                         op1=ALU.subtract),
                     reads=[Bt3, B_magic], writes=[Bt3])
            for i in range(IC):
                tt, Bt3, _w = wsub[i // 8]
                pt = psb[6 + i % 2]
                pBt = B_ps[6 + i % 2]
                P.op(T, lambda e, tt=tt, i=i, pt=pt: e.transpose(
                    out=pt[:, 0:128], in_=tt[:, (i % 8) * 128:(i % 8 + 1) * 128],
                    identity=ident_t[:]), reads=[Bt3, B_ident], writes=[pBt])
                P.op(S, lambda e, i=i, j=j, pt=pt: e.activation(
                    out=wdq_t[i][:, j * 128:(j + 1) * 128], in_=pt[:, 0:128],
                    func=AF.Copy), reads=[pBt], writes=[B_wdq[i]])

        for ic in range(IC):
            if ic + 2 < IC:
                load6(ic + 2)
            if ic + 1 < IC:
                prep6q(ic + 1)
            comp6(ic)
            if ic + 1 < IC:
                prep6t(ic + 1)
        big_close(bufs6)
        P.op(V, lambda e, n=24 + sidx[0]: e.tensor_reduce(
            out=mxb_t[:, 21:22], in_=mxb_t[:, 24:n], axis=AX.X, op=ALU.max),
            reads=[B_mxb], writes=[B_mxb])
        P.op(T, lambda e: e.transpose(out=psb[5][0:1, 0:128], in_=mxb_t[:, 21:22],
                                      identity=ident_t[:]),
             reads=[B_mxb, B_ident], writes=[B_ps[5]])
        Br11 = rowbuf(11)
        r11 = rows_t[0:1, 4 * TB + 1:4 * TB + 2]
        P.op(V, lambda e: e.tensor_reduce(out=r11, in_=psb[5][0:1, 0:128],
                                          axis=AX.X, op=ALU.max),
             reads=[B_ps[5]], writes=[Br11])
        P.dma(Y, ar3i, r11, reads=[Br11], writes=[B_ar["3i"]])
        P.op(G, lambda e: e.collective_compute(
            "AllReduce", ALU.max, replica_groups=[list(range(NC))], ins=[ar3i], outs=[ar3o]),
            reads=[B_ar["3i"]], writes=[B_ar["3o"]], sem="coll")
        P.dma(G, tsc_c, ar3o.partition_broadcast(128), reads=[B_ar["3o"]], writes=[B_tsc])
        P.op(V, lambda e: e.tensor_tensor(out=tsc_c, in0=tsc_c, in1=sigd_c, op=ALU.mult),
             reads=[B_tsc, B_sigd], writes=[B_tsc])
        P.op(V, lambda e: e.tensor_scalar(out=s3_c, in0=tsc_c, scalar1=1.0 / 127.0,
                                          scalar2=None, op0=ALU.mult),
             reads=[B_tsc], writes=[B_s3])
        P.op(V, lambda e: e.reciprocal(out=rs3_c, in_=tsc_c), reads=[B_tsc], writes=[B_rs3])
        P.op(V, lambda e: e.tensor_scalar(out=rs3_c, in0=rs3_c, scalar1=127.0, scalar2=None,
                                          op0=ALU.mult), reads=[B_rs3], writes=[B_rs3])
        P.op(V, lambda e: e.tensor_scalar(out=s3_c, in0=s3_c, scalar1=1.0 / 127.0,
                                          scalar2=None, op0=ALU.mult),
             reads=[B_s3], writes=[B_s3])
        for b in range(NB):
            for hc in range(DC):
                mxc = mxb_t[:, 64 + b * DC + hc:65 + b * DC + hc]
                P.op(V, lambda e, b=b, hc=hc, mxc=mxc: e.scalar_tensor_tensor(
                    out=mxc, in0=mdcols[:, hc:hc + 1], scalar=s3_c, in1=mcolv(5, b, hc),
                    op0=ALU.mult, op1=ALU.mult),
                    reads=[B_md, B_s3, B_modc], writes=[B_mxb])

        B_siq = [B_rbh[i] for i in range(11)]

        def siq_ap(i):
            return rbh_t[i % 11][0:128, (i // 11) * 512:(i // 11 + 1) * 512]

        for b in range(NB):
            for th in range(2):
                x2buf = {}
                for i in range(IC):
                    P.dma(Y, siq_ap(i),
                          sisp[(b * IC + i) * 128:(b * IC + i + 1) * 128,
                               th * 512:(th + 1) * 512],
                          reads=[B_sid[b * IC + i]], writes=[B_siq[i % 11]])
                    if i >= IC - 3:
                        hcp = i - (IC - 3)
                        k7 = xri[0] % 3
                        xri[0] += 1
                        t7 = P.dma(Y, xr_t[k7][0:128, 0:512],
                                   x2sp[(b * DC + hcp) * 128:(b * DC + hcp + 1) * 128,
                                        th * 512:(th + 1) * 512],
                                   writes=[B_xr[k7]],
                                   extra=B_x2d[b * DC + hcp].rwaits())
                        B_x2d[b * DC + hcp].did_r(t7)
                        x2buf[hcp] = (xr_t[k7], B_xr[k7])
                    P.op(S, lambda e, i=i: e.activation(
                        out=t512_t[i % 2][:], in_=siq_ap(i), func=AF.Identity, bias=magic,
                        scale=rs3_c), reads=[B_siq[i % 11], B_rs3, B_magic],
                        writes=[B_t512[i % 2]])
                    P.op(V, lambda e, i=i: e.tensor_scalar(
                        out=t512_t[2 + i % 2][:], in0=t512_t[i % 2][:], scalar1=magic,
                        scalar2=127.0, op0=ALU.subtract, op1=ALU.min),
                        reads=[B_t512[i % 2], B_magic], writes=[B_t512[2 + i % 2]])
                    P.op(V, lambda e, i=i: e.tensor_scalar(
                        out=siq_ap(i), in0=t512_t[2 + i % 2][:], scalar1=-128.0, scalar2=None,
                        op0=ALU.max), reads=[B_t512[2 + i % 2]], writes=[B_siq[i % 11]])
                    for hc in range(DC):
                        P.op(T, lambda e, i=i, hc=hc: e.matmul(
                            out=psb[hc][:], lhsT=wdq_t[i][0:128, hc * 128:(hc + 1) * 128],
                            rhs=siq_ap(i), start=(i == 0), stop=(i == IC - 1)),
                            reads=[B_wdq[i], B_siq[i % 11]],
                            writes=[B_ps[hc]] if i == 0 or i == IC - 1 else [])
                for hc in range(DC):
                    pd = psb[hc]
                    pB = B_ps[hc]
                    if hc in x2buf:
                        buf, Bb = x2buf[hc]
                    else:
                        k7 = xri[0] % 3
                        xri[0] += 1
                        buf, Bb = xr_t[k7], B_xr[k7]
                        t7 = P.dma(Y, buf[0:128, 0:512],
                                   x2sp[(b * DC + hc) * 128:(b * DC + hc + 1) * 128,
                                        th * 512:(th + 1) * 512],
                                   writes=[Bb], extra=B_x2d[b * DC + hc].rwaits())
                        B_x2d[b * DC + hc].did_r(t7)
                    P.op(V, lambda e, b=b, hc=hc, buf=buf, pd=pd: e.scalar_tensor_tensor(
                        out=t512_t[4 + hc % 2][:], in0=pd[:],
                        scalar=mxb_t[:, 64 + b * DC + hc:65 + b * DC + hc],
                        in1=buf[0:128, 0:512], op0=ALU.mult, op1=ALU.add),
                        reads=[pB, Bb, B_mxb], writes=[B_t512[4 + hc % 2]])
                    P.dma(G, out_d[(b * DC + hc) * 128:(b * DC + hc + 1) * 128,
                                   th * 512:(th + 1) * 512],
                          t512_t[4 + hc % 2][:], reads=[B_t512[4 + hc % 2]])
        P.barrier()

        @block.sync
        def _(e):
            for f in P.qs["sync"]:
                f(e)

        @block.vector
        def _(e):
            for f in P.qs["vector"]:
                f(e)

        @block.scalar
        def _(e):
            for f in P.qs["scalar"]:
                f(e)

        @block.tensor
        def _(e):
            for f in P.qs["tensor"]:
                f(e)

        @block.gpsimd
        def _(e):
            for f in P.qs["gpsimd"]:
                f(e)

    return nc


_NC_CACHE = [None]


def kernel(L0_attn_x, L0_attn_Wq, L0_attn_bq, L0_attn_Wk, L0_attn_bk, L0_attn_Wv,
           L0_attn_bv, L0_attn_Wo, L0_attn_bo, L0_ffn_clip_x, L0_ffn_clip_down,
           L0_ffn_w_gate, L0_ffn_w_up, L0_ffn_w_down, L0_c, L0_wmod):
    cc = np.ascontiguousarray
    x = np.asarray(L0_attn_x, dtype=np.float32)
    xfm = cc(x.transpose(0, 2, 1))
    wqT = cc(np.asarray(L0_attn_Wq, np.float32).T)
    wkT = cc(np.asarray(L0_attn_Wk, np.float32).T)
    wvT = cc(np.asarray(L0_attn_Wv, np.float32).T)
    woT = cc(np.asarray(L0_attn_Wo, np.float32).T)
    bq = cc(np.asarray(L0_attn_bq, np.float32).reshape(1, D))
    bk = cc(np.asarray(L0_attn_bk, np.float32).reshape(1, D))
    bv = cc(np.asarray(L0_attn_bv, np.float32).reshape(1, D))
    bo = cc(np.asarray(L0_attn_bo, np.float32).reshape(1, D))
    wg = np.asarray(L0_ffn_w_gate, np.float32)
    wu = np.asarray(L0_ffn_w_up, np.float32)
    wd = np.asarray(L0_ffn_w_down, np.float32)
    wg_n = cc(wg)
    wu_n = cc(wu)
    wd_n = cc(wd)
    cT = cc(np.asarray(L0_c, np.float32).T)
    wmod = np.asarray(L0_wmod, np.float32)
    clipx = cc(np.asarray(L0_ffn_clip_x, np.float32).reshape(1, 1))
    clipd = cc(np.asarray(L0_ffn_clip_down, np.float32).reshape(1, 1))

    if _NC_CACHE[0] is None:
        _NC_CACHE[0] = build_program()
    nc = _NC_CACHE[0]

    in_maps = []
    for c in range(NC):
        sl = slice(c * TB, (c + 1) * TB)
        in_maps.append({
            "x": cc(xfm[:, :, sl]).reshape(NB * D, TB),
            "wqT": wqT, "wkT": wkT, "wvT": wvT, "woT": woT,
            "bq": bq, "bk": bk, "bv": bv, "bo": bo,
            "wg": wg_n, "wu": wu_n, "wdn": wd_n,
            "cT": cc(cT[c * 128:(c + 1) * 128, :]),
            "wmods": cc(wmod[c * 128:(c + 1) * 128, :]),
            "clipx": clipx, "clipd": clipd,
        })
    res = run_bass_kernel_spmd(nc, in_maps, core_ids=list(range(NC)))
    parts = [np.asarray(res.results[c]["out"]).reshape(NB, D, TB) for c in range(NC)]
    full = np.concatenate(parts, axis=2)
    return np.ascontiguousarray(full.transpose(0, 2, 1))


# version 28
# speedup vs baseline: 1.0214x; 1.0109x over previous
import numpy as np
import concourse.bass as bass
import concourse.mybir as mybir
from concourse.bass_utils import run_bass_kernel_spmd

F32 = mybir.dt.float32
BF16 = mybir.dt.bfloat16
I32 = mybir.dt.int32
AF = mybir.ActivationFunctionType
ALU = mybir.AluOpType
AX = mybir.AxisListType

NC = 8
D = 1024
DC = 8
II = 2816
IC = 22
TB = 1024
NB = 2
MAGIC_K = float(1.5 * (2.0**82 - 2.0**53))
MAGIC_LO = float(1.5 * 2.0**23)

ENGINES = ("sync", "vector", "scalar", "tensor", "gpsimd")
V, S, T, G, Y = "vector", "scalar", "tensor", "gpsimd", "sync"


class Buf:
    __slots__ = ("wr", "rd")

    def __init__(self):
        self.wr = None
        self.rd = []

    def rwaits(self):
        return [self.wr] if self.wr else []

    def wwaits(self):
        return ([self.wr] if self.wr else []) + self.rd

    def did_r(self, tok):
        self.rd.append(tok)

    def did_w(self, tok):
        self.wr = tok
        self.rd = []


DMA_ROT = {"sync": 16, "gpsimd": 12}


class Prog:
    def __init__(self, nc):
        self.nc = nc
        self.qs = {e: [] for e in ENGINES}
        self.cnt = {}
        self.handles = {}
        self.watermark = {e: {} for e in ENGINES}
        self.sem_names = []
        self.dmarr = {e: 0 for e in ENGINES}

    def sem(self, name):
        self.cnt[name] = 0
        self.sem_names.append(name)

    def _waits(self, eng, waits):
        merged = {}
        for w in waits:
            if w is None:
                continue
            s, v = w
            if v <= 0:
                continue
            if merged.get(s, -1) < v:
                merged[s] = v
        for s, v in sorted(merged.items()):
            if self.watermark[eng].get(s, -1) >= v:
                continue
            self.watermark[eng][s] = v
            self.qs[eng].append(lambda e, s=s, v=v: e.wait_ge(self.handles[s], v))

    def op(self, eng, build, reads=(), writes=(), extra=(), sem=None, n=1):
        waits = list(extra)
        for b in reads:
            waits += b.rwaits()
        for b in writes:
            waits += b.wwaits()
        self._waits(eng, waits)
        s = sem if sem is not None else eng
        self.cnt[s] += n
        tok = (s, self.cnt[s])
        self.qs[eng].append(lambda e, b=build, s=s, n=n: b(e).then_inc(self.handles[s], n))
        for b in reads:
            b.did_r(tok)
        for b in writes:
            b.did_w(tok)
        return tok

    def dma(self, eng, out, in_, reads=(), writes=(), extra=(), sem=None, slow=False):
        if slow:
            bld = lambda e, o=out, i=in_: e.dma_start(out=o, in_=i,
                                                      allow_slow_non_contiguous=True)
        else:
            bld = lambda e, o=out, i=in_: e.dma_start(out=o, in_=i)
        if sem is None:
            r = DMA_ROT[eng]
            s = "%sq%d" % (eng[:2], self.dmarr[eng] % r)
            self.dmarr[eng] += 1
            self._waits(eng, [(s, self.cnt[s])])
            sem = s
        return self.op(eng, bld, reads=reads, writes=writes, extra=extra, sem=sem, n=16)

    def mark(self, eng):
        return (eng, self.cnt[eng])

    def barrier(self):
        toks = [(s, self.cnt[s]) for s in self.sem_names]
        for e in ENGINES:
            self._waits(e, toks)


def build_program():
    nc = bass.Bass("TRN2", target_bir_lowering=False, debug=False)
    ap = lambda t: t.ap()
    x_d = ap(nc.dram_tensor("x", [NB * D, TB], F32, kind="ExternalInput"))
    wqT_d = ap(nc.dram_tensor("wqT", [D, D], F32, kind="ExternalInput"))
    wkT_d = ap(nc.dram_tensor("wkT", [D, D], F32, kind="ExternalInput"))
    wvT_d = ap(nc.dram_tensor("wvT", [D, D], F32, kind="ExternalInput"))
    woT_d = ap(nc.dram_tensor("woT", [D, D], F32, kind="ExternalInput"))
    bq_d = ap(nc.dram_tensor("bq", [1, D], F32, kind="ExternalInput"))
    bk_d = ap(nc.dram_tensor("bk", [1, D], F32, kind="ExternalInput"))
    bv_d = ap(nc.dram_tensor("bv", [1, D], F32, kind="ExternalInput"))
    bo_d = ap(nc.dram_tensor("bo", [1, D], F32, kind="ExternalInput"))
    wg_d = ap(nc.dram_tensor("wg", [II, D], F32, kind="ExternalInput"))
    wu_d = ap(nc.dram_tensor("wu", [II, D], F32, kind="ExternalInput"))
    wdn_d = ap(nc.dram_tensor("wdn", [D, II], F32, kind="ExternalInput"))
    cT_d = ap(nc.dram_tensor("cT", [128, NB], F32, kind="ExternalInput"))
    wmods_d = ap(nc.dram_tensor("wmods", [128, 6 * D], F32, kind="ExternalInput"))
    clipx_d = ap(nc.dram_tensor("clipx", [1, 1], F32, kind="ExternalInput"))
    clipd_d = ap(nc.dram_tensor("clipd", [1, 1], F32, kind="ExternalInput"))
    out_d = ap(nc.dram_tensor("out", [NB * D, TB], F32, kind="ExternalOutput"))

    ar0i = ap(nc.dram_tensor("ar0i", [128, 96], F32))
    ar0o = ap(nc.dram_tensor("ar0o", [128, 96], F32))
    kvari = ap(nc.dram_tensor("kvari", [NB * DC * 128, 1025], BF16))
    kvaro = ap(nc.dram_tensor("kvaro", [NB * DC * 128, 1025], BF16))
    ar2i = ap(nc.dram_tensor("ar2i", [1, 1], F32))
    ar2o = ap(nc.dram_tensor("ar2o", [1, 1], F32))
    ar3i = ap(nc.dram_tensor("ar3i", [1, 1], F32))
    ar3o = ap(nc.dram_tensor("ar3o", [1, 1], F32))
    rows_d = ap(nc.dram_tensor("rows_d", [16, TB], F32))
    scal_d = ap(nc.dram_tensor("scal_d", [8, 1], F32))
    x2sp = ap(nc.dram_tensor("x2sp", [NB * D, TB], F32))
    sisp = ap(nc.dram_tensor("sisp", [NB * IC * 128, TB], BF16))

    P = Prog(nc)
    for s in ("sync", "vector", "scalar", "tensor", "gpsimd", "coll"):
        P.sem(s)
    for e, r in DMA_ROT.items():
        for i in range(r):
            P.sem("%sq%d" % (e[:2], i))

    from contextlib import ExitStack
    with ExitStack() as stk:
        sbt = lambda name, shape, dt: stk.enter_context(nc.sbuf_tensor(name, shape, dt))
        xr_t = [sbt(f"xr{i}", [128, TB], F32) for i in range(3)]
        abA_t = sbt("abA", [128, TB], F32)
        abB_t = sbt("abB", [128, TB], F32)
        rz_t = sbt("rz", [128, TB], F32)
        big_t = sbt("big", [128, 5632], F32)
        t512_t = [sbt(f"t512_{i}", [128, 512], F32) for i in range(6)]
        t1k_t = [sbt(f"t1k_{i}", [128, TB], F32) for i in range(3)]
        kvt_t = [sbt(f"kvt{i}", [128, 1025], BF16) for i in range(2)]
        rows_t = sbt("rows", [1, 4 * TB + 2], F32)
        smallc_t = sbt("smallc", [128, 64], F32)
        ident_t = sbt("ident", [128, 128], F32)
        io1_t = sbt("io1", [128, 128], I32)
        io2_t = sbt("io2", [128, 128], I32)
        modp_t = sbt("modp", [128, 96], F32)
        modc_t = sbt("modc", [128, 96], F32)
        siluc_t = sbt("siluc", [128, 4], F32)
        mcol_t = sbt("mcol", [128, 128], F32)
        mxb_t = sbt("mxb", [128, 128], F32)
        rbh_t = [sbt(f"rbh{i}", [128, TB], BF16) for i in range(16)]
        rbk_t = [sbt(f"rbk{i}", [128, TB], BF16) for i in range(16)]
        rbw_t = [sbt(f"rbw{i}", [128, TB], BF16) for i in range(16)]
        rbx_t = sbt("rbx", [128, 4800], BF16)
        bsc_t = [sbt(f"bsc{i}", [128, TB], BF16) for i in range(2)]
        psA = [stk.enter_context(nc.psum_tensor(f"pA{k}", [128, 1024], F32))
               for k in range(3)]
        psT = [stk.enter_context(nc.psum_tensor(f"pT{k}", [128, 512], F32))
               for k in range(2)]
        psb = [psA[0][:, 0:512], psA[0][:, 512:1024],
               psA[1][:, 0:512], psA[1][:, 512:1024],
               psA[2][:, 0:512], psA[2][:, 512:1024],
               psT[0][:, 0:512], psT[1][:, 0:512]]
        for s in P.sem_names:
            P.handles[s] = stk.enter_context(nc.semaphore(f"sem_{s}"))
        block = stk.enter_context(nc.Block())

        B_xr = [Buf() for _ in range(3)]
        B_abA, B_abB, B_rz = Buf(), Buf(), Buf()
        big_guard = [[]]

        def big_bufs(k):
            toks = list(big_guard[0])
            bufs = [Buf() for _ in range(k)]
            return bufs, toks

        def big_close(bufs):
            toks = []
            for b in bufs:
                toks += b.wwaits()
            big_guard[0] = toks

        B_t512 = [Buf() for _ in range(6)]
        B_t1k = [Buf() for _ in range(3)]
        B_kvt = [Buf() for _ in range(2)]
        B_rows = {}

        def rowbuf(i):
            if i not in B_rows:
                B_rows[i] = Buf()
            return B_rows[i]

        B_col = {}
        COLIDX = {}

        def col(name):
            if name not in COLIDX:
                COLIDX[name] = len(COLIDX)
                B_col[name] = Buf()
            i = COLIDX[name]
            return smallc_t[:, i:i + 1], B_col[name]

        B_ident, B_io1, B_io2 = Buf(), Buf(), Buf()
        B_modp, B_modc, B_siluc, B_mcol, B_mxb = Buf(), Buf(), Buf(), Buf(), Buf()
        B_rbh = [Buf() for _ in range(16)]
        B_rbk = [Buf() for _ in range(16)]
        B_rbw = [Buf() for _ in range(16)]
        B_bq, B_bk, B_bv, B_bo = Buf(), Buf(), Buf(), Buf()
        B_o512, B_o128, B_oc = Buf(), Buf(), Buf()
        B_ks = [Buf() for _ in range(DC)]
        B_bsc = [Buf() for _ in range(2)]
        B_ps = [Buf() for _ in range(8)]
        B_x2d = [Buf() for _ in range(NB * DC)]
        B_sid = [Buf() for _ in range(NB * IC)]
        B_rowd = [Buf() for _ in range(16)]
        B_scald = [Buf() for _ in range(8)]
        B_ar = {k: Buf() for k in ("0i", "0o", "1i", "1o", "2i", "2o", "3i", "3o")}

        bq_r = rbx_t[0:1, 0:1024]
        bk_r = rbx_t[0:1, 1024:2048]
        bv_r = rbx_t[0:1, 2048:3072]
        bo_r = rbx_t[0:1, 3072:4096]
        ones_r512 = rbx_t[0:1, 4096:4608]
        ones_r128 = rbx_t[0:1, 4608:4736]
        ones_c = rbx_t[:, 4740:4741]
        ks_cs = [rbx_t[:, 4750 + i:4751 + i] for i in range(DC)]

        magic, B_magic = col("magic")
        eps_c, B_eps = col("eps")
        sigx_c, B_sigx = col("sigx")
        sigd_c, B_sigd = col("sigd")
        rs2_c, B_rs2 = col("rs2")
        s2_c, B_s2 = col("s2")
        rs3_c, B_rs3 = col("rs3")
        s3_c, B_s3 = col("s3")
        kst_c, B_kst = col("kst")
        tsc_c, B_tsc = col("tsc")

        def row(i, n=TB):
            return rows_t[0:1, i * TB:i * TB + n]

        def mcolv(g, b, c):
            j = (g * 8 + c) * 2 + b
            return modc_t[:, j:j + 1]

        P.op(G, lambda e: e.iota(out=io1_t[:], pattern=[[0, 128]], channel_multiplier=1),
             writes=[B_io1])
        P.op(G, lambda e: e.iota(out=io2_t[:], pattern=[[1, 128]], channel_multiplier=0),
             writes=[B_io2])
        P.op(V, lambda e: e.tensor_tensor(out=ident_t[:], in0=io1_t[:], in1=io2_t[:],
                                          op=ALU.is_equal),
             reads=[B_io1, B_io2], writes=[B_ident])
        P.op(V, lambda e: e.memset(mxb_t[:], 0.0), writes=[B_mxb])
        P.op(V, lambda e: e.memset(mcol_t[:], 0.0), writes=[B_mcol])
        P.op(V, lambda e: e.memset(magic, 1.0), writes=[B_magic])
        P.op(V, lambda e: e.tensor_scalar(out=eps_c, in0=magic, scalar1=float(2.0**-30),
                                          scalar2=1.0, op0=ALU.add, op1=ALU.subtract),
             reads=[B_magic], writes=[B_eps])
        P.op(V, lambda e: e.tensor_scalar(out=magic, in0=eps_c, scalar1=MAGIC_K,
                                          scalar2=MAGIC_LO, op0=ALU.mult, op1=ALU.add),
             reads=[B_eps], writes=[B_magic])
        P.op(V, lambda e: e.memset(ones_r512, 1.0), writes=[B_o512])
        P.op(V, lambda e: e.memset(ones_r128, 1.0), writes=[B_o128])
        P.op(V, lambda e: e.memset(ones_c, 1.0), writes=[B_oc])
        P.dma(G, bq_r, bq_d, writes=[B_bq])
        P.dma(G, bk_r, bk_d, writes=[B_bk])
        P.dma(G, bv_r, bv_d, writes=[B_bv])
        P.dma(G, bo_r, bo_d, writes=[B_bo])
        P.dma(Y, t512_t[0][0:1, 0:1], clipx_d, writes=[B_t512[0]])
        P.dma(Y, t512_t[0][0:1, 1:2], clipd_d, writes=[B_t512[0]])
        P.op(S, lambda e: e.activation(out=t512_t[1][0:1, 0:1], in_=t512_t[0][0:1, 0:1],
                                       func=AF.Sigmoid), reads=[B_t512[0]], writes=[B_t512[1]])
        P.op(S, lambda e: e.activation(out=t512_t[1][0:1, 1:2], in_=t512_t[0][0:1, 1:2],
                                       func=AF.Sigmoid), reads=[B_t512[0]], writes=[B_t512[1]])
        P.dma(Y, scal_d[0:1, 0:1], t512_t[1][0:1, 0:1], reads=[B_t512[1]], writes=[B_scald[0]])
        P.dma(Y, scal_d[1:2, 0:1], t512_t[1][0:1, 1:2], reads=[B_t512[1]], writes=[B_scald[1]])
        P.dma(Y, sigx_c, scal_d[0:1, 0:1].partition_broadcast(128),
              reads=[B_scald[0]], writes=[B_sigx])
        P.dma(Y, sigd_c, scal_d[1:2, 0:1].partition_broadcast(128),
              reads=[B_scald[1]], writes=[B_sigd])

        B_bqc, B_boc = Buf(), Buf()
        bqc_cols = mcol_t[:, 120:128]
        boc_cols = mxb_t[:, 80:88]
        P.dma(G, bqc_cols, bq_d.rearrange("one (a p) -> p (a one)", p=128),
              writes=[B_bqc], slow=True)
        P.dma(G, boc_cols, bo_d.rearrange("one (a p) -> p (a one)", p=128),
              writes=[B_boc], slow=True)

        P.dma(Y, siluc_t[:, 0:2], cT_d, writes=[B_siluc])
        P.op(S, lambda e: e.activation(out=siluc_t[:, 2:4], in_=siluc_t[:, 0:2], func=AF.Silu),
             reads=[B_siluc], writes=[B_siluc])
        bw, bg0 = big_bufs(1)
        B_wmod = bw[0]
        for half in range(2):
            P.dma(Y, big_t[:, 0:3072], wmods_d[:, half * 3072:(half + 1) * 3072],
                  writes=[B_wmod], extra=bg0)
            for j in range(24):
                jg = half * 24 + j
                pb = psb[6 + (jg % 2)]
                pB = B_ps[6 + (jg % 2)]
                P.op(T, lambda e, j=j, pb=pb: e.matmul(
                    out=pb[:, 0:2], lhsT=big_t[:, j * 128:(j + 1) * 128], rhs=siluc_t[:, 2:4],
                    start=True, stop=True),
                    reads=[B_wmod, B_siluc], writes=[pB])
                P.op(V, lambda e, jg=jg, pb=pb: e.tensor_copy(
                    out=modp_t[:, 2 * jg:2 * jg + 2], in_=pb[:, 0:2]),
                    reads=[pB], writes=[B_modp])
        big_close([B_wmod])
        P.dma(G, ar0i, modp_t[:], reads=[B_modp], writes=[B_ar["0i"]])
        P.op(G, lambda e: e.collective_compute(
            "AllReduce", ALU.add, replica_groups=[list(range(NC))], ins=[ar0i], outs=[ar0o]),
            reads=[B_ar["0i"]], writes=[B_ar["0o"]], sem="coll")
        P.dma(G, modc_t[:], ar0o, reads=[B_ar["0o"]], writes=[B_modc])

        def clampg(g0):
            sl = modc_t[:, g0 * 16:g0 * 16 + 16]
            a, b_, c_, d_ = (t1k_t[0][:, k * 16:(k + 1) * 16] for k in range(4))
            P.op(V, lambda e: e.tensor_scalar(out=a, in0=sl, scalar1=-1.0, scalar2=None,
                                              op0=ALU.mult), reads=[B_modc], writes=[B_t1k[0]])
            P.op(V, lambda e: e.tensor_tensor(out=b_, in0=sl, in1=a, op=ALU.max),
                 reads=[B_modc, B_t1k[0]], writes=[B_t1k[0]])
            P.op(V, lambda e: e.tensor_scalar(out=b_, in0=b_, scalar1=0.25, scalar2=None,
                                              op0=ALU.max), reads=[B_t1k[0]], writes=[B_t1k[0]])
            P.op(V, lambda e: e.tensor_scalar(out=c_, in0=sl, scalar1=0.0, scalar2=None,
                                              op0=ALU.is_ge), reads=[B_modc], writes=[B_t1k[0]])
            P.op(V, lambda e: e.tensor_scalar(out=d_, in0=c_, scalar1=2.0, scalar2=-1.0,
                                              op0=ALU.mult, op1=ALU.add),
                 reads=[B_t1k[0]], writes=[B_t1k[0]])
            P.op(V, lambda e: e.tensor_tensor(out=sl, in0=b_, in1=d_, op=ALU.mult),
                 reads=[B_t1k[0]], writes=[B_modc])
        clampg(2)
        clampg(5)
        P.op(V, lambda e: e.tensor_scalar(out=modc_t[:, 16:32], in0=modc_t[:, 16:32],
                                          scalar1=1.0, scalar2=None, op0=ALU.add),
             reads=[B_modc], writes=[B_modc])
        P.op(V, lambda e: e.tensor_scalar(out=modc_t[:, 64:80], in0=modc_t[:, 64:80],
                                          scalar1=1.0, scalar2=None, op0=ALU.add),
             reads=[B_modc], writes=[B_modc])

        for i in range(DC):
            P.dma(G, rbw_t[i][:], wkT_d[i * 128:(i + 1) * 128, :], writes=[B_rbw[i]])
        for i in range(DC):
            P.dma(G, rbw_t[8 + i][:], wvT_d[i * 128:(i + 1) * 128, :], writes=[B_rbw[8 + i]])

        xri = [0]

        def xload(src_d, idx, extra_src=None):
            k = xri[0] % 3
            xri[0] += 1
            ex = [] if extra_src is None else extra_src.rwaits()
            tok = P.dma(Y, xr_t[k][:], src_d[idx * 128:(idx + 1) * 128, :],
                        writes=[B_xr[k]], extra=ex)
            if extra_src is not None:
                extra_src.did_r(tok)
            return xr_t[k][:], B_xr[k]

        def ln_pass(b, src_d, sc0, sc1, Bsc0, Bsc1, src_bufs, drow):
            sum_sl = [psb[0][0:1, 0:512], psb[1][0:1, 0:512]]
            sq_sl = [psb[2][0:1, 0:512], psb[3][0:1, 0:512]]
            scp = [(sc0, sc1, Bsc0, Bsc1), (rbh_t[12][:], rbh_t[13][:], B_rbh[12], B_rbh[13])]
            for c in range(DC):
                s0, s1, Bs0, Bs1 = scp[c % 2]
                buf, Bb = xload(src_d, b * DC + c,
                                src_bufs[b * DC + c] if src_bufs else None)
                P.op(S, lambda e, buf=buf, s=s0: e.activation(out=s[:], in_=buf, func=AF.Copy),
                     reads=[Bb], writes=[Bs0])
                P.op(S, lambda e, buf=buf, s=s1: e.activation(out=s[:], in_=buf, func=AF.Square),
                     reads=[Bb], writes=[Bs1])
                for h in range(2):
                    P.op(T, lambda e, h=h, c=c, s=s0: e.matmul(
                        out=sum_sl[h], lhsT=ones_c, rhs=s[:, h * 512:(h + 1) * 512],
                        start=(c == 0), stop=(c == DC - 1)),
                        reads=[Bs0, B_oc], writes=[B_ps[h]] if c == DC - 1 or c == 0 else [])
                    P.op(T, lambda e, h=h, c=c, s=s1: e.matmul(
                        out=sq_sl[h], lhsT=ones_c, rhs=s[:, h * 512:(h + 1) * 512],
                        start=(c == 0), stop=(c == DC - 1)),
                        reads=[Bs1, B_oc], writes=[B_ps[2 + h]] if c == DC - 1 or c == 0 else [])
            r_M, r_T, r_A, r_B = (row(i) for i in range(4))
            Bm, Bt, BA, BB = (rowbuf(i) for i in range(4))
            P.op(V, lambda e: e.tensor_copy(out=r_M[:, 0:1024], in_=psA[0][0:1, 0:1024]),
                 reads=[B_ps[0], B_ps[1]], writes=[Bm])
            P.op(V, lambda e: e.tensor_copy(out=r_T[:, 0:1024], in_=psA[1][0:1, 0:1024]),
                 reads=[B_ps[2], B_ps[3]], writes=[Bt])
            P.op(V, lambda e: e.tensor_scalar(out=r_M, in0=r_M, scalar1=1.0 / D,
                                              scalar2=None, op0=ALU.mult),
                 reads=[Bm], writes=[Bm])
            P.op(V, lambda e: e.tensor_tensor(out=r_A, in0=r_M, in1=r_M, op=ALU.mult),
                 reads=[Bm], writes=[BA])
            P.op(V, lambda e: e.scalar_tensor_tensor(out=r_T, in0=r_T, scalar=1.0 / D,
                                                     in1=r_A, op0=ALU.mult, op1=ALU.subtract),
                 reads=[Bt, BA], writes=[Bt])
            P.op(V, lambda e: e.tensor_scalar(out=r_T, in0=r_T, scalar1=1e-5, scalar2=None,
                                              op0=ALU.add), reads=[Bt], writes=[Bt])
            BdA, BdB = rowbuf(100 + drow), rowbuf(101 + drow)
            P.dma(G, rows_d[drow:drow + 1, :], r_T, reads=[Bt], writes=[BdA])
            P.dma(G, rows_d[drow + 1:drow + 2, :], r_M, reads=[Bm], writes=[BdB])

        def ln_bcast(drow):
            BdA, BdB = rowbuf(100 + drow), rowbuf(101 + drow)
            P.dma(G, abA_t[:], rows_d[drow:drow + 1, :].partition_broadcast(128),
                  reads=[BdA], writes=[B_abA])
            P.dma(G, abB_t[:], rows_d[drow + 1:drow + 2, :].partition_broadcast(128),
                  reads=[BdB], writes=[B_abB])
            P.op(S, lambda e: e.activation(out=abA_t[:], in_=abA_t[:], func=AF.Sqrt),
                 reads=[B_abA], writes=[B_abA])
            P.op(V, lambda e: e.reciprocal(out=abA_t[:], in_=abA_t[:]),
                 reads=[B_abA], writes=[B_abA])
            P.op(V, lambda e: e.scalar_tensor_tensor(out=abB_t[:], in0=abB_t[:],
                                                     scalar=-1.0, in1=abA_t[:],
                                                     op0=ALU.mult, op1=ALU.mult),
                 reads=[B_abB, B_abA], writes=[B_abB])

        for b in range(NB):
            ln_pass(b, x_d, bsc_t[0], bsc_t[1], B_bsc[0], B_bsc[1], None, 4 + 2 * b)
        for b in range(NB):
            ln_bcast(4 + 2 * b)
            for c in range(DC):
                buf, Bb = xload(x_d, b * DC + c)
                P.op(S, lambda e, b=b, c=c: e.activation(
                    out=t1k_t[2][:], in_=abB_t[:], func=AF.Identity,
                    scale=mcolv(1, b, c), bias=mcolv(0, b, c)),
                    reads=[B_abB, B_modc], writes=[B_t1k[2]])
                P.op(G, lambda e, buf=buf: e.tensor_tensor(
                    out=t1k_t[1][:], in0=buf, in1=abA_t[:], op=ALU.mult),
                    reads=[Bb, B_abA], writes=[B_t1k[1]])
                P.op(V, lambda e, b=b, c=c: e.scalar_tensor_tensor(
                    out=rbh_t[b * DC + c][:], in0=t1k_t[1][:], scalar=mcolv(1, b, c),
                    in1=t1k_t[2][:], op0=ALU.mult, op1=ALU.add),
                    reads=[B_t1k[1], B_t1k[2], B_modc], writes=[B_rbh[b * DC + c]])

        kv_dma_toks = [[], []]
        B_kvd = [Buf() for _ in range(NB * DC)]
        B_ar1 = [Buf(), Buf()]
        for b in range(NB):
            for n in range(DC):
                p2 = n % 2
                kb = [psb[0 + 4 * p2], psb[1 + 4 * p2]]
                kB = [B_ps[0 + 4 * p2], B_ps[1 + 4 * p2]]
                vb = [psb[2 + 4 * p2], psb[3 + 4 * p2]]
                vB = [B_ps[2 + 4 * p2], B_ps[3 + 4 * p2]]
                kfull = psA[0][:] if p2 == 0 else psA[2][:]
                for i in range(DC):
                    lhs = rbh_t[b * DC + i][:, n * 128:(n + 1) * 128]
                    for h in range(2):
                        P.op(T, lambda e, lhs=lhs, h=h, i=i, kb=kb: e.matmul(
                            out=kb[h][:], lhsT=lhs, rhs=rbw_t[i][:, h * 512:(h + 1) * 512],
                            start=(i == 0), stop=False),
                            reads=[B_rbh[b * DC + i], B_rbw[i]],
                            writes=[kB[h]] if i == 0 else [])
                        P.op(T, lambda e, lhs=lhs, h=h, i=i, vb=vb: e.matmul(
                            out=vb[h][:], lhsT=lhs, rhs=rbw_t[8 + i][:, h * 512:(h + 1) * 512],
                            start=(i == 0), stop=False),
                            reads=[B_rbh[b * DC + i], B_rbw[8 + i]],
                            writes=[vB[h]] if i == 0 else [])
                for h in range(2):
                    P.op(T, lambda e, h=h, kb=kb: e.matmul(
                        out=kb[h][:], lhsT=ones_r128, rhs=bk_r[0:1, h * 512:(h + 1) * 512],
                        start=False, stop=True),
                        reads=[B_o128, B_bk], writes=[kB[h]])
                    P.op(T, lambda e, h=h, vb=vb: e.matmul(
                        out=vb[h][:], lhsT=ones_r128, rhs=bv_r[0:1, h * 512:(h + 1) * 512],
                        start=False, stop=True),
                        reads=[B_o128, B_bv], writes=[vB[h]])
                km_, Bkm = (t1k_t[0], B_t1k[0]) if p2 == 0 else (t1k_t[2], B_t1k[2])
                ke_, Bke = (t1k_t[1], B_t1k[1]) if p2 == 0 else (rz_t, B_rz)
                P.op(V, lambda e, kfull=kfull, km_=km_: e.tensor_scalar(
                    out=km_[:], in0=kfull, scalar1=0.0, scalar2=None, op0=ALU.min),
                    reads=kB, writes=[Bkm])
                P.op(S, lambda e, km_=km_, ke_=ke_: e.activation(
                    out=ke_[:], in_=km_[:], func=AF.Exp),
                    reads=[Bkm], writes=[Bke])
                P.op(V, lambda e, n=n, kfull=kfull, ke_=ke_: e.scalar_tensor_tensor(
                    out=rbk_t[n][:], in0=kfull, scalar=0.0,
                    in1=ke_[:], op0=ALU.max, op1=ALU.add),
                    reads=kB + [Bke], writes=[B_rbk[n]])
                if p2 == 0:
                    P.op(S, lambda e, n=n: e.activation(
                        out=rbk_t[8 + n][:], in_=psA[1][:],
                        func=AF.Copy), reads=vB, writes=[B_rbk[8 + n]])
                else:
                    for h in range(2):
                        P.op(S, lambda e, n=n, h=h, vb=vb: e.activation(
                            out=rbk_t[8 + n][:, h * 512:(h + 1) * 512], in_=vb[h][:],
                            func=AF.Copy), reads=[vB[h]], writes=[B_rbk[8 + n]])
            for dc in range(DC):
                p2 = dc % 2
                wb = [psb[4 - 4 * p2], psb[5 - 4 * p2]]
                wB = [B_ps[4 - 4 * p2], B_ps[5 - 4 * p2]]
                wfull = psA[2][:] if p2 == 0 else psA[0][:]
                ksb = psb[6 + p2]
                ksB = B_ps[6 + p2]
                for n in range(DC):
                    lhs = rbk_t[n][:, dc * 128:(dc + 1) * 128]
                    for h in range(2):
                        P.op(T, lambda e, lhs=lhs, n=n, h=h, wb=wb: e.matmul(
                            out=wb[h][:], lhsT=lhs, rhs=rbk_t[8 + n][:, h * 512:(h + 1) * 512],
                            start=(n == 0), stop=(n == DC - 1)),
                            reads=[B_rbk[n], B_rbk[8 + n]],
                            writes=[wB[h]] if n == 0 or n == DC - 1 else [])
                    P.op(T, lambda e, lhs=lhs, n=n, ksb=ksb: e.matmul(
                        out=ksb[:, 0:1], lhsT=lhs, rhs=ones_c,
                        start=(n == 0), stop=(n == DC - 1)),
                        reads=[B_rbk[n], B_oc],
                        writes=[ksB] if n == 0 or n == DC - 1 else [])
                kvb_ = kvt_t[dc % 2][:]
                Bk = B_kvt[dc % 2]
                P.op(V, lambda e, kvb_=kvb_, wfull=wfull: e.tensor_copy(
                    out=kvb_[0:128, 0:1024], in_=wfull),
                     reads=wB, writes=[Bk])
                P.op(V, lambda e, kvb_=kvb_, ksb=ksb: e.tensor_copy(
                    out=kvb_[0:128, 1024:1025], in_=ksb[:, 0:1]),
                     reads=[ksB], writes=[Bk])
                kv_dma_toks[b].append(P.dma(
                    Y, kvari[(b * DC + dc) * 128:(b * DC + dc + 1) * 128, :], kvb_,
                    reads=[Bk], writes=[B_kvd[b * DC + dc]]))
            P.op(G, lambda e, b=b: e.collective_compute(
                "AllReduce", ALU.add, replica_groups=[list(range(NC))],
                ins=[kvari[b * DC * 128:(b + 1) * DC * 128, :]],
                outs=[kvaro[b * DC * 128:(b + 1) * DC * 128, :]]),
                reads=B_kvd[b * DC:(b + 1) * DC], writes=[B_ar1[b]], sem="coll",
                extra=kv_dma_toks[b])

        for i in range(DC):
            P.dma(G, rbw_t[i][:], wqT_d[i * 128:(i + 1) * 128, :], writes=[B_rbw[i]])
        for b in range(NB):
            for oc in range(DC):
                pq = [psb[0 + 2 * (oc % 2)], psb[1 + 2 * (oc % 2)]]
                pB = [B_ps[0 + 2 * (oc % 2)], B_ps[1 + 2 * (oc % 2)]]
                for i in range(DC):
                    for h in range(2):
                        P.op(T, lambda e, i=i, oc=oc, h=h, b=b, pq=pq: e.matmul(
                            out=pq[h][:], lhsT=rbw_t[i][:, oc * 128:(oc + 1) * 128],
                            rhs=rbh_t[b * DC + i][:, h * 512:(h + 1) * 512],
                            start=(i == 0), stop=(i == DC - 1)),
                            reads=[B_rbw[i], B_rbh[b * DC + i]],
                            writes=[pB[h]] if i == 0 or i == DC - 1 else [])
                pqf = psA[oc % 2][:]
                ts_, Bts = (t1k_t[0], B_t1k[0]) if oc % 2 == 0 else (t1k_t[2], B_t1k[2])
                te_, Bte = (t1k_t[1], B_t1k[1]) if oc % 2 == 0 else (rz_t, B_rz)
                bqc = bqc_cols[:, oc:oc + 1]
                P.op(V, lambda e, pqf=pqf, ts_=ts_, bqc=bqc: e.tensor_scalar(
                    out=ts_[:], in0=pqf, scalar1=bqc, scalar2=None, op0=ALU.add),
                    reads=pB + [B_bqc], writes=[Bts])
                P.op(V, lambda e, ts_=ts_, te_=te_: e.tensor_scalar(
                    out=te_[:], in0=ts_[:], scalar1=0.0, scalar2=None, op0=ALU.min),
                    reads=[Bts], writes=[Bte])
                P.op(S, lambda e, te_=te_: e.activation(
                    out=te_[:], in_=te_[:], func=AF.Exp),
                    reads=[Bte], writes=[Bte])
                P.op(V, lambda e, b=b, oc=oc, ts_=ts_, te_=te_: e.scalar_tensor_tensor(
                    out=rbk_t[b * DC + oc][:], in0=ts_[:],
                    scalar=0.0, in1=te_[:], op0=ALU.max, op1=ALU.add),
                    reads=[Bts, Bte], writes=[B_rbk[b * DC + oc]])

        for i in range(DC):
            P.dma(G, rbw_t[8 + i][:], woT_d[i * 128:(i + 1) * 128, :], writes=[B_rbw[8 + i]])
        for b in range(NB):
            kvt_set = rbh_t[0:8] if b == 0 else rbw_t[0:8]
            B_kvt_set = B_rbh[0:8] if b == 0 else B_rbw[0:8]
            kvld = []
            for dc in range(DC):
                t = P.dma(G, kvt_set[dc][:],
                          kvaro[(b * DC + dc) * 128:(b * DC + dc + 1) * 128, 0:1024],
                          reads=[B_ar1[b]], writes=[B_kvt_set[dc]])
                kvld.append(t)
            ks8 = mcol_t[:, 112:120]
            P.dma(G, ks8,
                  kvaro[b * DC * 128:(b + 1) * DC * 128, 1024:1025]
                  .rearrange("(a p) one -> p (a one)", p=128),
                  reads=[B_ar1[b]], writes=[B_kst], slow=True)
            P.op(V, lambda e, ks8=ks8: e.tensor_scalar(
                out=ks8, in0=ks8, scalar1=1e-6, scalar2=None, op0=ALU.add),
                reads=[B_kst], writes=[B_kst])
            P.op(V, lambda e, ks8=ks8: e.tensor_copy(out=rbx_t[:, 4750:4758], in_=ks8),
                 reads=[B_kst], writes=B_ks)
            zsl = [psb[4][0:1, 0:512], psb[5][0:1, 0:512]]
            for dc in range(DC):
                for h in range(2):
                    P.op(T, lambda e, dc=dc, h=h, b=b, zsl=zsl: e.matmul(
                        out=zsl[h], lhsT=ks_cs[dc], rhs=rbk_t[b * DC + dc][:, h * 512:(h + 1) * 512],
                        start=(dc == 0), stop=(dc == DC - 1)),
                        reads=[B_ks[dc], B_rbk[b * DC + dc]],
                        writes=[B_ps[4 + h]] if dc == 0 or dc == DC - 1 else [])
            Bz8, Bz9 = rowbuf(0), rowbuf(1)
            P.op(V, lambda e: e.tensor_copy(out=row(0)[:, 0:1024], in_=psA[2][0:1, 0:1024]),
                 reads=[B_ps[4], B_ps[5]], writes=[Bz8])
            P.op(V, lambda e: e.reciprocal(out=row(1), in_=row(0)), reads=[Bz8], writes=[Bz9])
            Bz9d = rowbuf(109)
            P.dma(Y, rows_d[9:10, :], row(1), reads=[Bz9], writes=[Bz9d])
            P.dma(Y, rz_t[:], rows_d[9:10, :].partition_broadcast(128),
                  reads=[Bz9d], writes=[B_rz])
            for ec in range(DC):
                pc = [psb[0 + 2 * (ec % 2)], psb[1 + 2 * (ec % 2)]]
                pB = [B_ps[0 + 2 * (ec % 2)], B_ps[1 + 2 * (ec % 2)]]
                for dc in range(DC):
                    for h in range(2):
                        P.op(T, lambda e, dc=dc, ec=ec, h=h, b=b, pc=pc,
                             kv=kvt_set: e.matmul(
                            out=pc[h][:], lhsT=kv[dc][:, ec * 128:(ec + 1) * 128],
                            rhs=rbk_t[b * DC + dc][:, h * 512:(h + 1) * 512],
                            start=(dc == 0), stop=(dc == DC - 1)),
                            reads=[B_kvt_set[dc], B_rbk[b * DC + dc]],
                            writes=[pB[h]] if dc == 0 or dc == DC - 1 else [])
                P.op(V, lambda e, ec=ec: e.tensor_tensor(
                    out=rbh_t[8 + ec][:], in0=psA[ec % 2][:],
                    in1=rz_t[:], op=ALU.mult),
                    reads=pB + [B_rz], writes=[B_rbh[8 + ec]])
            for oc in range(DC):
                po = [psb[0 + 2 * (oc % 2)], psb[1 + 2 * (oc % 2)]]
                pB = [B_ps[0 + 2 * (oc % 2)], B_ps[1 + 2 * (oc % 2)]]
                for ec in range(DC):
                    for h in range(2):
                        P.op(T, lambda e, ec=ec, oc=oc, h=h, po=po: e.matmul(
                            out=po[h][:], lhsT=rbw_t[8 + ec][:, oc * 128:(oc + 1) * 128],
                            rhs=rbh_t[8 + ec][:, h * 512:(h + 1) * 512],
                            start=(ec == 0), stop=(ec == DC - 1)),
                            reads=[B_rbw[8 + ec], B_rbh[8 + ec]],
                            writes=[pB[h]] if ec == 0 or ec == DC - 1 else [])
                buf, Bb = xload(x_d, b * DC + oc)
                x2t = t1k_t[2][:]
                P.op(V, lambda e, b=b, oc=oc, x2t=x2t: e.tensor_scalar(
                    out=x2t, in0=psA[oc % 2][:], scalar1=boc_cols[:, oc:oc + 1],
                    scalar2=mcolv(2, b, oc), op0=ALU.add, op1=ALU.mult),
                    reads=pB + [B_boc, B_modc], writes=[B_t1k[2]])
                P.op(V, lambda e, x2t=x2t, buf=buf: e.tensor_tensor(
                    out=x2t, in0=x2t, in1=buf[0:128, 0:1024], op=ALU.add),
                    reads=[B_t1k[2], Bb], writes=[B_t1k[2]])
                P.dma(Y, x2sp[(b * DC + oc) * 128:(b * DC + oc + 1) * 128, :], x2t,
                      reads=[B_t1k[2]], writes=[B_x2d[b * DC + oc]])

        nmx = 0
        for b in range(NB):
            ln_pass(b, x2sp, bsc_t[0], bsc_t[1], B_bsc[0], B_bsc[1], B_x2d, 4 + 2 * b)
        for b in range(NB):
            ln_bcast(4 + 2 * b)
            for c in range(DC):
                buf, Bb = xload(x2sp, b * DC + c, B_x2d[b * DC + c])
                P.op(S, lambda e, b=b, c=c: e.activation(
                    out=t1k_t[2][:], in_=abB_t[:], func=AF.Identity,
                    scale=mcolv(4, b, c), bias=mcolv(3, b, c)),
                    reads=[B_abB, B_modc], writes=[B_t1k[2]])
                P.op(G, lambda e, buf=buf: e.tensor_tensor(
                    out=t1k_t[1][:], in0=buf, in1=abA_t[:], op=ALU.mult),
                    reads=[Bb, B_abA], writes=[B_t1k[1]])
                P.op(V, lambda e, b=b, c=c: e.scalar_tensor_tensor(
                    out=rbk_t[b * DC + c][:], in0=t1k_t[1][:], scalar=mcolv(4, b, c),
                    in1=t1k_t[2][:], op0=ALU.mult, op1=ALU.add),
                    reads=[B_t1k[1], B_t1k[2], B_modc], writes=[B_rbk[b * DC + c]])
                P.op(V, lambda e, b=b, c=c, nmx=nmx: e.tensor_reduce(
                    out=mxb_t[:, nmx:nmx + 1], in_=rbk_t[b * DC + c][:], axis=AX.X,
                    op=ALU.max, apply_absolute_value=True),
                    reads=[B_rbk[b * DC + c]], writes=[B_mxb])
                nmx += 1
        bufs6, bg6 = big_bufs(4)
        Bfg = bufs6[0:2]
        Bfu = bufs6[2:4]
        fg = [big_t[:, 0:1024], big_t[:, 1024:2048]]
        fu = [big_t[:, 2048:3072], big_t[:, 3072:4096]]
        Bsg = [B_rbk[0], B_rbk[1]]
        Bsu = [B_rbk[2], B_rbk[3]]
        Bsi = [B_rbk[4], B_rbk[5], B_rbk[6], B_rbk[7]]
        B_mg = [Buf(), Buf()]
        B_mu = [Buf(), Buf()]
        B_rg6 = [Buf(), Buf()]
        B_ru6 = [Buf(), Buf()]
        B_sg2 = [Buf(), Buf()]
        B_su2 = [Buf(), Buf()]
        sidx = [0]
        first6 = [bg6]

        def load6(ic):
            p6 = ic % 2
            P.dma(Y, fg[p6], wg_d[ic * 128:(ic + 1) * 128, :], writes=[Bfg[p6]],
                  extra=first6[0])
            P.dma(G, fu[p6], wu_d[ic * 128:(ic + 1) * 128, :], writes=[Bfu[p6]],
                  extra=first6[0])
            first6[0] = []

        def cols6(p6):
            return (mcol_t[:, 0 + p6:1 + p6], mcol_t[:, 2 + p6:3 + p6],
                    mcol_t[:, 4 + p6:5 + p6], mcol_t[:, 6 + p6:7 + p6],
                    mcol_t[:, 8 + p6:9 + p6], mcol_t[:, 10 + p6:11 + p6])

        QG = [bsc_t[0], rbk_t[4]]
        QU = [bsc_t[1], rbk_t[5]]
        BQG = [B_bsc[0], B_rbk[4]]
        BQU = [B_bsc[1], B_rbk[5]]

        def prep6q(ic):
            p6 = ic % 2
            fgb, fub = fg[p6], fu[p6]
            Bg, Bu = Bfg[p6], Bfu[p6]
            mgc, muc, rgc, ruc, sgc, suc = cols6(p6)
            for (fb, Bf, mc, Bm, rc, Br) in (
                    (fgb, Bg, mgc, B_mg[p6], rgc, B_rg6[p6]),
                    (fub, Bu, muc, B_mu[p6], ruc, B_ru6[p6])):
                P.op(V, lambda e, fb=fb, mc=mc: e.tensor_reduce(
                    out=mc, in_=fb, axis=AX.X, op=ALU.max, apply_absolute_value=True),
                    reads=[Bf], writes=[Bm])
                P.op(V, lambda e, mc=mc, rc=rc: e.reciprocal(out=rc, in_=mc),
                     reads=[Bm], writes=[Br])
                P.op(V, lambda e, fb=fb, rc=rc: e.tensor_scalar(
                    out=fb, in0=fb, scalar1=rc, scalar2=127.0, op0=ALU.mult, op1=ALU.mult),
                    reads=[Bf, Br], writes=[Bf])
                P.op(V, lambda e, fb=fb: e.tensor_scalar(
                    out=fb, in0=fb, scalar1=magic, scalar2=magic, op0=ALU.add,
                    op1=ALU.subtract), reads=[Bf, B_magic], writes=[Bf])

        def prep6t(ic):
            p6 = ic % 2
            fgb, fub = fg[p6], fu[p6]
            Bg, Bu = Bfg[p6], Bfu[p6]
            qg, qu = QG[p6], QU[p6]
            Bqg, Bqu = BQG[p6], BQU[p6]
            for hc in range(DC):
                for (fb, Bf, qt, Bq, pk) in ((fgb, Bg, qg, Bqg, 0), (fub, Bu, qu, Bqu, 1)):
                    pt = psT[pk]
                    pBt = B_ps[6 + pk]
                    P.op(T, lambda e, fb=fb, hc=hc, pt=pt: e.transpose(
                        out=pt[:, (hc % 4) * 128:(hc % 4 + 1) * 128],
                        in_=fb[:, hc * 128:(hc + 1) * 128],
                        identity=ident_t[:]), reads=[Bf, B_ident], writes=[pBt])
                    if hc % 4 == 3:
                        hg = hc // 4
                        P.op(S, lambda e, hg=hg, qt=qt, pt=pt: e.activation(
                            out=qt[:, hg * 512:(hg + 1) * 512], in_=pt[:, 0:512],
                            func=AF.Copy), reads=[pBt], writes=[Bq])

        def comp6(ic):
            p6 = ic % 2
            mgc, muc, rgc, ruc, sgc, suc = cols6(p6)
            qg, qu = QG[p6], QU[p6]
            Bqg, Bqu = BQG[p6], BQU[p6]
            P.op(V, lambda e, mgc=mgc, sgc=sgc: e.tensor_scalar(
                out=sgc, in0=mgc, scalar1=s2_c, scalar2=1.0 / 127.0, op0=ALU.mult,
                op1=ALU.mult), reads=[B_mg[p6], B_s2], writes=[B_sg2[p6]])
            P.op(V, lambda e, muc=muc, suc=suc: e.tensor_scalar(
                out=suc, in0=muc, scalar1=s2_c, scalar2=1.0 / 127.0, op0=ALU.mult,
                op1=ALU.mult), reads=[B_mu[p6], B_s2], writes=[B_su2[p6]])
            for b in range(NB):
                gA, uA = (psA[0], psA[1]) if b == 0 else (psA[2], psA[0])
                gb0, ub0 = (0, 2) if b == 0 else (4, 0)
                pBg = [B_ps[gb0], B_ps[gb0 + 1]]
                pBu = [B_ps[ub0], B_ps[ub0 + 1]]
                for hc in range(DC):
                    for h in range(2):
                        P.op(T, lambda e, hc=hc, h=h, b=b, qg=qg, gA=gA: e.matmul(
                            out=gA[:, h * 512:(h + 1) * 512],
                            lhsT=qg[:, hc * 128:(hc + 1) * 128],
                            rhs=rbh_t[b * DC + hc][:, h * 512:(h + 1) * 512],
                            start=(hc == 0), stop=(hc == DC - 1)),
                            reads=[Bqg, B_rbh[b * DC + hc]],
                            writes=[pBg[h]] if hc == 0 or hc == DC - 1 else [])
                        P.op(T, lambda e, hc=hc, h=h, b=b, qu=qu, uA=uA: e.matmul(
                            out=uA[:, h * 512:(h + 1) * 512],
                            lhsT=qu[:, hc * 128:(hc + 1) * 128],
                            rhs=rbh_t[b * DC + hc][:, h * 512:(h + 1) * 512],
                            start=(hc == 0), stop=(hc == DC - 1)),
                            reads=[Bqu, B_rbh[b * DC + hc]],
                            writes=[pBu[h]] if hc == 0 or hc == DC - 1 else [])
                t1x = t1k_t[sidx[0] % 2]
                Bt1x = B_t1k[sidx[0] % 2]
                P.op(S, lambda e, sgc=sgc, t1x=t1x, gA=gA: e.activation(
                    out=t1x[:], in_=gA[:], func=AF.Silu, scale=sgc),
                    reads=pBg + [B_sg2[p6]], writes=[Bt1x])
                P.op(V, lambda e, suc=suc, t1x=t1x, uA=uA: e.scalar_tensor_tensor(
                    out=t1x[:], in0=uA[:], scalar=suc, in1=t1x[:],
                    op0=ALU.mult, op1=ALU.mult),
                    reads=pBu + [B_su2[p6], Bt1x], writes=[Bt1x])
                P.op(V, lambda e, k=24 + sidx[0], t1x=t1x: e.tensor_reduce(
                    out=mxb_t[:, k:k + 1], in_=t1x[:], axis=AX.X, op=ALU.max,
                    apply_absolute_value=True), reads=[Bt1x], writes=[B_mxb])
                P.dma(G, sisp[(b * IC + ic) * 128:(b * IC + ic + 1) * 128, :], t1x[:],
                      reads=[Bt1x], writes=[B_sid[b * IC + ic]])
                sidx[0] += 1

        load6(0)
        load6(1)
        prep6q(0)
        prep6t(0)
        P.op(V, lambda e, n=nmx: e.tensor_reduce(
            out=mxb_t[:, 20:21], in_=mxb_t[:, 0:n], axis=AX.X, op=ALU.max),
            reads=[B_mxb], writes=[B_mxb])
        P.op(T, lambda e: e.transpose(out=psb[5][0:1, 0:128], in_=mxb_t[:, 20:21],
                                      identity=ident_t[:]),
             reads=[B_mxb, B_ident], writes=[B_ps[5]])
        Br10 = rowbuf(10)
        r10 = rows_t[0:1, 4 * TB:4 * TB + 1]
        P.op(V, lambda e: e.tensor_reduce(out=r10, in_=psb[5][0:1, 0:128],
                                          axis=AX.X, op=ALU.max),
             reads=[B_ps[5]], writes=[Br10])
        P.dma(Y, ar2i, r10, reads=[Br10], writes=[B_ar["2i"]])
        P.op(G, lambda e: e.collective_compute(
            "AllReduce", ALU.max, replica_groups=[list(range(NC))], ins=[ar2i], outs=[ar2o]),
            reads=[B_ar["2i"]], writes=[B_ar["2o"]], sem="coll")
        P.dma(G, tsc_c, ar2o.partition_broadcast(128), reads=[B_ar["2o"]], writes=[B_tsc])
        P.op(V, lambda e: e.tensor_tensor(out=tsc_c, in0=tsc_c, in1=sigx_c, op=ALU.mult),
             reads=[B_tsc, B_sigx], writes=[B_tsc])
        P.op(V, lambda e: e.tensor_scalar(out=s2_c, in0=tsc_c, scalar1=1.0 / 127.0,
                                          scalar2=None, op0=ALU.mult),
             reads=[B_tsc], writes=[B_s2])
        P.op(V, lambda e: e.reciprocal(out=rs2_c, in_=tsc_c), reads=[B_tsc], writes=[B_rs2])
        P.op(V, lambda e: e.tensor_scalar(out=rs2_c, in0=rs2_c, scalar1=127.0, scalar2=None,
                                          op0=ALU.mult), reads=[B_rs2], writes=[B_rs2])
        for b in range(NB):
            for c in range(DC):
                P.op(S, lambda e, b=b, c=c: e.activation(
                    out=t1k_t[2][:], in_=rbk_t[b * DC + c][:], func=AF.Identity,
                    bias=magic, scale=rs2_c),
                    reads=[B_rbk[b * DC + c], B_rs2, B_magic], writes=[B_t1k[2]])
                P.op(V, lambda e: e.tensor_scalar(
                    out=t1k_t[2][:], in0=t1k_t[2][:], scalar1=magic, scalar2=127.0,
                    op0=ALU.subtract, op1=ALU.min),
                    reads=[B_t1k[2], B_magic], writes=[B_t1k[2]])
                P.op(V, lambda e, b=b, c=c: e.tensor_scalar(
                    out=rbh_t[b * DC + c][:], in0=t1k_t[2][:], scalar1=-128.0, scalar2=None,
                    op0=ALU.max), reads=[B_t1k[2]], writes=[B_rbh[b * DC + c]])

        wdq_t = [rbw_t[i] for i in range(16)] + [rbk_t[8 + i] for i in range(6)]
        B_wdq = [B_rbw[i] for i in range(16)] + [B_rbk[8 + i] for i in range(6)]
        mdcols = mcol_t[:, 96:104]
        B_md = Buf()
        rd7_c, B_rd7 = col("rd7")
        md_sc = mcol_t[:, 104:107]
        B_mdsc = Buf()
        wsub = [(t1k_t[0][:, 0:1024], B_t1k[0], 1024),
                (t1k_t[1][:, 0:1024], B_t1k[1], 1024),
                (t1k_t[2][:, 0:768], B_t1k[2], 768)]
        for j in range(DC):
            for si3, (tt, Bt3, w) in enumerate(wsub):
                P.dma(Y if si3 % 2 == 0 else G, tt,
                      wdn_d[j * 128:(j + 1) * 128, si3 * 1024:si3 * 1024 + w],
                      writes=[Bt3])
                P.op(V, lambda e, tt=tt, si3=si3: e.tensor_reduce(
                    out=md_sc[:, si3:si3 + 1], in_=tt, axis=AX.X, op=ALU.max,
                    apply_absolute_value=True), reads=[Bt3], writes=[B_mdsc])
            mdc = mdcols[:, j:j + 1]
            P.op(V, lambda e, mdc=mdc: e.tensor_tensor(
                out=mdc, in0=md_sc[:, 0:1], in1=md_sc[:, 1:2], op=ALU.max),
                reads=[B_mdsc], writes=[B_md])
            P.op(V, lambda e, mdc=mdc: e.tensor_tensor(
                out=mdc, in0=mdc, in1=md_sc[:, 2:3], op=ALU.max),
                reads=[B_md, B_mdsc], writes=[B_md])
            P.op(V, lambda e, mdc=mdc: e.reciprocal(out=rd7_c, in_=mdc),
                 reads=[B_md], writes=[B_rd7])
            for (tt, Bt3, w) in wsub:
                P.op(V, lambda e, tt=tt: e.tensor_scalar(out=tt, in0=tt, scalar1=rd7_c,
                                                         scalar2=127.0, op0=ALU.mult,
                                                         op1=ALU.mult),
                     reads=[Bt3, B_rd7], writes=[Bt3])
                P.op(V, lambda e, tt=tt: e.tensor_scalar(out=tt, in0=tt, scalar1=magic,
                                                         scalar2=magic, op0=ALU.add,
                                                         op1=ALU.subtract),
                     reads=[Bt3, B_magic], writes=[Bt3])
            for i in range(IC):
                tt, Bt3, _w = wsub[i // 8]
                pt = psb[6 + i % 2]
                pBt = B_ps[6 + i % 2]
                P.op(T, lambda e, tt=tt, i=i, pt=pt: e.transpose(
                    out=pt[:, 0:128], in_=tt[:, (i % 8) * 128:(i % 8 + 1) * 128],
                    identity=ident_t[:]), reads=[Bt3, B_ident], writes=[pBt])
                P.op(S, lambda e, i=i, j=j, pt=pt: e.activation(
                    out=wdq_t[i][:, j * 128:(j + 1) * 128], in_=pt[:, 0:128],
                    func=AF.Copy), reads=[pBt], writes=[B_wdq[i]])

        for ic in range(IC):
            if ic + 2 < IC:
                load6(ic + 2)
            if ic + 1 < IC:
                prep6q(ic + 1)
            comp6(ic)
            if ic + 1 < IC:
                prep6t(ic + 1)
        big_close(bufs6)
        P.op(V, lambda e, n=24 + sidx[0]: e.tensor_reduce(
            out=mxb_t[:, 21:22], in_=mxb_t[:, 24:n], axis=AX.X, op=ALU.max),
            reads=[B_mxb], writes=[B_mxb])
        P.op(T, lambda e: e.transpose(out=psb[5][0:1, 0:128], in_=mxb_t[:, 21:22],
                                      identity=ident_t[:]),
             reads=[B_mxb, B_ident], writes=[B_ps[5]])
        Br11 = rowbuf(11)
        r11 = rows_t[0:1, 4 * TB + 1:4 * TB + 2]
        P.op(V, lambda e: e.tensor_reduce(out=r11, in_=psb[5][0:1, 0:128],
                                          axis=AX.X, op=ALU.max),
             reads=[B_ps[5]], writes=[Br11])
        P.dma(Y, ar3i, r11, reads=[Br11], writes=[B_ar["3i"]])
        P.op(G, lambda e: e.collective_compute(
            "AllReduce", ALU.max, replica_groups=[list(range(NC))], ins=[ar3i], outs=[ar3o]),
            reads=[B_ar["3i"]], writes=[B_ar["3o"]], sem="coll")
        P.dma(G, tsc_c, ar3o.partition_broadcast(128), reads=[B_ar["3o"]], writes=[B_tsc])
        P.op(V, lambda e: e.tensor_tensor(out=tsc_c, in0=tsc_c, in1=sigd_c, op=ALU.mult),
             reads=[B_tsc, B_sigd], writes=[B_tsc])
        P.op(V, lambda e: e.tensor_scalar(out=s3_c, in0=tsc_c, scalar1=1.0 / 127.0,
                                          scalar2=None, op0=ALU.mult),
             reads=[B_tsc], writes=[B_s3])
        P.op(V, lambda e: e.reciprocal(out=rs3_c, in_=tsc_c), reads=[B_tsc], writes=[B_rs3])
        P.op(V, lambda e: e.tensor_scalar(out=rs3_c, in0=rs3_c, scalar1=127.0, scalar2=None,
                                          op0=ALU.mult), reads=[B_rs3], writes=[B_rs3])
        P.op(V, lambda e: e.tensor_scalar(out=s3_c, in0=s3_c, scalar1=1.0 / 127.0,
                                          scalar2=None, op0=ALU.mult),
             reads=[B_s3], writes=[B_s3])
        for b in range(NB):
            for hc in range(DC):
                mxc = mxb_t[:, 64 + b * DC + hc:65 + b * DC + hc]
                P.op(V, lambda e, b=b, hc=hc, mxc=mxc: e.scalar_tensor_tensor(
                    out=mxc, in0=mdcols[:, hc:hc + 1], scalar=s3_c, in1=mcolv(5, b, hc),
                    op0=ALU.mult, op1=ALU.mult),
                    reads=[B_md, B_s3, B_modc], writes=[B_mxb])

        B_siq = [B_rbh[i] for i in range(11)]

        def siq_ap(i):
            return rbh_t[i % 11][0:128, (i // 11) * 512:(i // 11 + 1) * 512]

        for b in range(NB):
            for th in range(2):
                x2buf = {}
                for i in range(IC):
                    P.dma(Y, siq_ap(i),
                          sisp[(b * IC + i) * 128:(b * IC + i + 1) * 128,
                               th * 512:(th + 1) * 512],
                          reads=[B_sid[b * IC + i]], writes=[B_siq[i % 11]])
                    if i >= IC - 3:
                        hcp = i - (IC - 3)
                        k7 = xri[0] % 3
                        xri[0] += 1
                        t7 = P.dma(Y, xr_t[k7][0:128, 0:512],
                                   x2sp[(b * DC + hcp) * 128:(b * DC + hcp + 1) * 128,
                                        th * 512:(th + 1) * 512],
                                   writes=[B_xr[k7]],
                                   extra=B_x2d[b * DC + hcp].rwaits())
                        B_x2d[b * DC + hcp].did_r(t7)
                        x2buf[hcp] = (xr_t[k7], B_xr[k7])
                    P.op(S, lambda e, i=i: e.activation(
                        out=t512_t[i % 2][:], in_=siq_ap(i), func=AF.Identity, bias=magic,
                        scale=rs3_c), reads=[B_siq[i % 11], B_rs3, B_magic],
                        writes=[B_t512[i % 2]])
                    P.op(V, lambda e, i=i: e.tensor_scalar(
                        out=t512_t[2 + i % 2][:], in0=t512_t[i % 2][:], scalar1=magic,
                        scalar2=127.0, op0=ALU.subtract, op1=ALU.min),
                        reads=[B_t512[i % 2], B_magic], writes=[B_t512[2 + i % 2]])
                    P.op(V, lambda e, i=i: e.tensor_scalar(
                        out=siq_ap(i), in0=t512_t[2 + i % 2][:], scalar1=-128.0, scalar2=None,
                        op0=ALU.max), reads=[B_t512[2 + i % 2]], writes=[B_siq[i % 11]])
                    for hc in range(DC):
                        P.op(T, lambda e, i=i, hc=hc: e.matmul(
                            out=psb[hc][:], lhsT=wdq_t[i][0:128, hc * 128:(hc + 1) * 128],
                            rhs=siq_ap(i), start=(i == 0), stop=(i == IC - 1)),
                            reads=[B_wdq[i], B_siq[i % 11]],
                            writes=[B_ps[hc]] if i == 0 or i == IC - 1 else [])
                for hc in range(DC):
                    pd = psb[hc]
                    pB = B_ps[hc]
                    if hc in x2buf:
                        buf, Bb = x2buf[hc]
                    else:
                        k7 = xri[0] % 3
                        xri[0] += 1
                        buf, Bb = xr_t[k7], B_xr[k7]
                        t7 = P.dma(Y, buf[0:128, 0:512],
                                   x2sp[(b * DC + hc) * 128:(b * DC + hc + 1) * 128,
                                        th * 512:(th + 1) * 512],
                                   writes=[Bb], extra=B_x2d[b * DC + hc].rwaits())
                        B_x2d[b * DC + hc].did_r(t7)
                    P.op(V, lambda e, b=b, hc=hc, buf=buf, pd=pd: e.scalar_tensor_tensor(
                        out=t512_t[4 + hc % 2][:], in0=pd[:],
                        scalar=mxb_t[:, 64 + b * DC + hc:65 + b * DC + hc],
                        in1=buf[0:128, 0:512], op0=ALU.mult, op1=ALU.add),
                        reads=[pB, Bb, B_mxb], writes=[B_t512[4 + hc % 2]])
                    P.dma(G, out_d[(b * DC + hc) * 128:(b * DC + hc + 1) * 128,
                                   th * 512:(th + 1) * 512],
                          t512_t[4 + hc % 2][:], reads=[B_t512[4 + hc % 2]])
        P.barrier()

        @block.sync
        def _(e):
            for f in P.qs["sync"]:
                f(e)

        @block.vector
        def _(e):
            for f in P.qs["vector"]:
                f(e)

        @block.scalar
        def _(e):
            for f in P.qs["scalar"]:
                f(e)

        @block.tensor
        def _(e):
            for f in P.qs["tensor"]:
                f(e)

        @block.gpsimd
        def _(e):
            for f in P.qs["gpsimd"]:
                f(e)

    return nc


_NC_CACHE = [None]


def kernel(L0_attn_x, L0_attn_Wq, L0_attn_bq, L0_attn_Wk, L0_attn_bk, L0_attn_Wv,
           L0_attn_bv, L0_attn_Wo, L0_attn_bo, L0_ffn_clip_x, L0_ffn_clip_down,
           L0_ffn_w_gate, L0_ffn_w_up, L0_ffn_w_down, L0_c, L0_wmod):
    cc = np.ascontiguousarray
    x = np.asarray(L0_attn_x, dtype=np.float32)
    xfm = cc(x.transpose(0, 2, 1))
    wqT = cc(np.asarray(L0_attn_Wq, np.float32).T)
    wkT = cc(np.asarray(L0_attn_Wk, np.float32).T)
    wvT = cc(np.asarray(L0_attn_Wv, np.float32).T)
    woT = cc(np.asarray(L0_attn_Wo, np.float32).T)
    bq = cc(np.asarray(L0_attn_bq, np.float32).reshape(1, D))
    bk = cc(np.asarray(L0_attn_bk, np.float32).reshape(1, D))
    bv = cc(np.asarray(L0_attn_bv, np.float32).reshape(1, D))
    bo = cc(np.asarray(L0_attn_bo, np.float32).reshape(1, D))
    wg = np.asarray(L0_ffn_w_gate, np.float32)
    wu = np.asarray(L0_ffn_w_up, np.float32)
    wd = np.asarray(L0_ffn_w_down, np.float32)
    wg_n = cc(wg)
    wu_n = cc(wu)
    wd_n = cc(wd)
    cT = cc(np.asarray(L0_c, np.float32).T)
    wmod = np.asarray(L0_wmod, np.float32)
    clipx = cc(np.asarray(L0_ffn_clip_x, np.float32).reshape(1, 1))
    clipd = cc(np.asarray(L0_ffn_clip_down, np.float32).reshape(1, 1))

    if _NC_CACHE[0] is None:
        _NC_CACHE[0] = build_program()
    nc = _NC_CACHE[0]

    in_maps = []
    for c in range(NC):
        sl = slice(c * TB, (c + 1) * TB)
        in_maps.append({
            "x": cc(xfm[:, :, sl]).reshape(NB * D, TB),
            "wqT": wqT, "wkT": wkT, "wvT": wvT, "woT": woT,
            "bq": bq, "bk": bk, "bv": bv, "bo": bo,
            "wg": wg_n, "wu": wu_n, "wdn": wd_n,
            "cT": cc(cT[c * 128:(c + 1) * 128, :]),
            "wmods": cc(wmod[c * 128:(c + 1) * 128, :]),
            "clipx": clipx, "clipd": clipd,
        })
    res = run_bass_kernel_spmd(nc, in_maps, core_ids=list(range(NC)))
    parts = [np.asarray(res.results[c]["out"]).reshape(NB, D, TB) for c in range(NC)]
    full = np.concatenate(parts, axis=2)
    return np.ascontiguousarray(full.transpose(0, 2, 1))
